# version 12
# baseline (speedup 1.0000x reference)
import math
import numpy as np
import concourse.bass as bass
import concourse.mybir as mybir
from concourse.bass_utils import run_bass_kernel_spmd

F32 = mybir.dt.float32
BF16 = mybir.dt.bfloat16
AF = mybir.ActivationFunctionType
ALU = mybir.AluOpType
AX = mybir.AxisListType

S = 2048
E = 2048
H = 16
D = 128
NT = 16
NHC = 64
SCALE = float(1.0 / math.sqrt(float(D)))
EPS = 1e-5
HGRP = 8
NGRP = NHC // HGRP

ENGINES = ("sync", "tensor", "vector", "scalar", "gpsimd")


class SemT:
    def __init__(self, h):
        self.h = h
        self.n = 0

    def tok(self, k):
        self.n += k
        return self.n


def build(debug=False):
    nc = bass.Bass("TRN2", target_bir_lowering=False)

    x_d = nc.dram_tensor("x", [S, E], F32, kind="ExternalInput").ap()
    qw_d = nc.dram_tensor("qw", [H, 128, 16, 128], F32, kind="ExternalInput").ap()
    kw_d = nc.dram_tensor("kw", [H, 128, 16, 128], F32, kind="ExternalInput").ap()
    vw_d = nc.dram_tensor("vw", [H, 128, 16, 128], F32, kind="ExternalInput").ap()
    ow_d = nc.dram_tensor("ow", [E, E], F32, kind="ExternalInput").ap()
    gw_d = nc.dram_tensor("gw", [NHC, 128, 16, 128], F32, kind="ExternalInput").ap()
    uw_d = nc.dram_tensor("uw", [NHC, 128, 16, 128], F32, kind="ExternalInput").ap()
    dw_d = nc.dram_tensor("dw", [NHC, 128, E], F32, kind="ExternalInput").ap()
    mask_d = nc.dram_tensor("mask", [128, 128], F32, kind="ExternalInput").ap()
    ident_d = nc.dram_tensor("ident", [128, 128], F32, kind="ExternalInput").ap()
    y_d = nc.dram_tensor("y", [S, E], F32, kind="ExternalOutput").ap()
    attn_scr = nc.dram_tensor("attn_scr", [NT, 128, H, 128], BF16).ap()
    x1_scr = nc.dram_tensor("x1_scr", [S, E], F32).ap()
    dbg = {}
    if debug:
        dbg["xn"] = nc.dram_tensor("d_xn", [128, 32768], BF16, kind="ExternalOutput").ap()
        dbg["at"] = nc.dram_tensor("d_at", [NT, 128, H, 128], BF16, kind="ExternalOutput").ap()
        dbg["x1"] = nc.dram_tensor("d_x1", [S, E], F32, kind="ExternalOutput").ap()
        dbg["xn2"] = nc.dram_tensor("d_xn2", [128, 16384], BF16, kind="ExternalOutput").ap()

    MAP_BYTES = 208896
    guard = nc.alloc_sbuf_tensor("manual_region", [128, MAP_BYTES], mybir.dt.uint8)
    SB_BASE = nc.lookup_mloc(guard).addr

    def sb(name, shape, dtype, off):
        nb = int(np.prod(shape[1:])) * (2 if dtype == BF16 else 4)
        assert off + nb <= MAP_BYTES, (name, off, nb)
        return nc.alloc_sbuf_tensor_at(name, shape, dtype, offset=SB_BASE + off).ap()

    bigA = sb("bigA", [128, 16384], F32, 0)
    bigA_bf = bigA.bitcast(BF16)
    Qb = [sb(f"Qb{i}", [128, 2048], BF16, 65536 + i * 4096) for i in range(2)]
    Kb = [sb(f"Kb{i}", [128, 2048], BF16, 73728 + i * 4096) for i in range(2)]
    Vtb = [sb(f"Vtb{i}", [128, 2048], BF16, 81920 + i * 4096) for i in range(2)]
    Vfb = sb("Vfb", [128, 2048], BF16, 90112)
    probs = sb("probs", [128, 2048], BF16, 94208)
    probsT = sb("probsT", [128, 2048], BF16, 98304)
    diag = sb("diag", [128, 128], BF16, 102400)
    atst = sb("atst", [128, 512], BF16, 102656)
    attC = [sb(f"attC{i}", [128, 2048], BF16, 114688 + i * 4096) for i in range(2)]
    xn2h = sb("xn2h", [128, 16384], BF16, 65536)
    ident_b = sb("ident_b", [128, 128], BF16, 103680)
    mask_b = sb("mask_b", [128, 128], BF16, 103936)
    st_negmean = sb("st_negmean", [128, 16], F32, 104192)
    st_ssq = sb("st_ssq", [128, 16], F32, 104256)
    st_lnv = sb("st_lnv", [128, 16], F32, 104320)
    st_rstd = sb("st_rstd", [128, 16], F32, 104384)
    st_negmax = sb("st_negmax", [128, 16], F32, 104448)
    st_negb = sb("st_negb", [128, 16], F32, 104512)
    st_ssum = sb("st_ssum", [128, 16], F32, 104576)
    st_lns = sb("st_lns", [128, 16], F32, 104640)
    st_rcp = sb("st_rcp", [128, 16], F32, 104704)
    epsb = sb("epsb", [128, 1], F32, 104768)
    st_pm = sb("st_pm", [128, 4], F32, 104832)
    probs2 = sb("probs2", [128, 2048], BF16, 104960)
    diag2 = sb("diag2", [128, 128], BF16, 109056)
    st_se = sb("st_se", [128, 4], F32, 109312)
    rbT1 = sb("rbT1", [1, 128], BF16, 109440)
    rbB = sb("rbB", [128, 512], BF16, 109824)
    ones1 = sb("ones1", [1, 128], BF16, 110848)
    ident_f = sb("ident_f", [128, 128], F32, 111104)
    wqb = sb("wqb", [128, 2048], BF16, 114688)
    wkb = sb("wkb", [128, 2048], BF16, 118784)
    wvb = sb("wvb", [128, 2048], BF16, 122880)
    dwg = sb("dwg", [128, 16384], BF16, 114688)
    xt = [sb(f"xt{i}", [128, 2048], F32, 147456 + i * 8192) for i in range(2)]
    gwb = [sb(f"gwb{i}", [128, 2048], BF16, 147456 + i * 4096) for i in range(2)]
    uwb = [sb(f"uwb{i}", [128, 2048], BF16, 155648 + i * 4096) for i in range(2)]
    x1t = [sb(f"x1t{i}", [128, 2048], F32, 163840 + i * 8192) for i in range(2)]
    hid = sb("hid", [128, 8192], BF16, 180224)
    sq = sb("sq", [128, 2048], F32, 180224)
    xntm = sb("xntm", [128, 2048], BF16, 196608)
    sil = sb("sil", [128, 1024], F32, 200704)

    ps_a = nc.alloc_psum_tensor("ps_a", [128, 2048], F32).ap()
    ps_b = [nc.alloc_psum_tensor(f"ps_b{i}", [128, 512], F32).ap() for i in range(2)]
    ps_t = nc.alloc_psum_tensor("ps_t", [128, 1024], BF16).ap()
    ps_c = nc.alloc_psum_tensor("ps_c", [128, 512], F32).ap()

    import contextlib
    stack = contextlib.ExitStack()
    sems = {}
    for name in ("pe", "dve", "act", "s_misc", "s_xa0", "s_xa1", "s_xa2",
                 "s_xa3", "s_qw", "s_kw",
                 "s_vw", "s_attw", "s_ow", "s_attr0", "s_attr1", "s_xc0", "s_xc1",
                 "s_x1w0", "s_x1w1", "s_x1r0", "s_x1r1", "s_dai",
                 "s_gw0", "s_gw1", "s_uw0", "s_uw1", "s_dw", "s_y", "s_dbg"):
        sems[name] = SemT(stack.enter_context(nc.semaphore(name)))
    pe, dve, act = sems["pe"], sems["dve"], sems["act"]

    steps = {e: [] for e in ENGINES}

    def OP(e):
        def deco(fn):
            steps[e].append(fn)
        return deco

    def barrier(extra=(), engines=ENGINES):
        targets = [(s.h, s.n) for s in (pe, dve, act)] + [(s.h, s.n) for s in extra]
        targets = [(h, n) for h, n in targets if n > 0]
        for e in engines:
            def f(eng, targets=tuple(targets)):
                for h, n in targets:
                    eng.wait_ge(h, n)
            steps[e].append(f)

    last = {
        "ps_t": None, "ps_a": None, "ps_b0": None, "ps_b1": None, "ps_c": None,
        "ps_f0": None, "ps_f1": None, "sil0": None, "sil1": None,
        "probs0": None, "probs1": None, "probsT": None,
        "diag0": None, "diag1": None, "atst": None, "st_se": None, "rbB": None,
        "xntm": None, "sil": None, "hid": None, "dwg": None,
        "xt0": None, "xt1": None, "x1t0": None, "x1t1": None,
        "wq": None, "wk": None, "wv": None,
        "gw0": None, "gw1": None, "uw0": None, "uw1": None,
        "attC0": None, "attC1": None,
    }

    def W(eng, *keys):
        for k in keys:
            v = last[k] if isinstance(k, str) else k
            if v:
                eng.wait_ge(v[0], v[1])

    xn_view = bigA_bf.rearrange("p (e n) -> p e n", e=16)

    @OP("gpsimd")
    def _(eng):
        eng.dma_start(out=ident_b, in_=ident_d).then_inc(sems["s_misc"].h, 16)
        eng.dma_start(out=mask_b, in_=mask_d).then_inc(sems["s_misc"].h, 16)
        eng.dma_start(out=ident_f, in_=ident_d).then_inc(sems["s_misc"].h, 16)
    t_misc = sems["s_misc"].tok(48)

    t_eps = dve.tok(2)

    @OP("vector")
    def _(eng):
        eng.memset(epsb[:], EPS).then_inc(dve.h, 1)
        eng.memset(ones1[:], 1.0).then_inc(dve.h, 1)

    ln1_tok = {}

    def emit_ln_tile(i, src_tile, t_src, src_sem, dst_view, dst_col0,
                     dst_war=None):
        t_red = dve.tok(1)
        t_nm = dve.tok(1)

        @OP("vector")
        def _(eng, i=i, src_tile=src_tile, t_src=t_src, src_sem=src_sem,
              t_red=t_red):
            eng.wait_ge(src_sem, t_src)
            eng.reduce_sum(out=st_negmean[:, i:i + 1], in_=src_tile[:], axis=AX.X).then_inc(dve.h, 1)
            eng.wait_ge(dve.h, t_red)
            eng.tensor_scalar_mul(out=st_negmean[:, i:i + 1], in0=st_negmean[:, i:i + 1],
                                  scalar1=-1.0 / E).then_inc(dve.h, 1)

        t_ssq = act.tok(1)
        t_lnv = act.tok(1)
        t_rstd = act.tok(1)

        @OP("scalar")
        def _(eng, i=i, src_tile=src_tile, t_nm=t_nm, t_ssq=t_ssq,
              t_lnv=t_lnv, t_eps=t_eps, war=last["xntm"]):
            eng.wait_ge(dve.h, t_nm)
            W(eng, war)
            eng.activation(out=xntm[:], in_=src_tile[:], func=AF.Square,
                           bias=st_negmean[:, i:i + 1],
                           accum_out=st_ssq[:, i:i + 1]).then_inc(act.h, 1)
            eng.wait_ge(act.h, t_ssq)
            eng.wait_ge(dve.h, t_eps)
            eng.activation(out=st_lnv[:, i:i + 1], in_=st_ssq[:, i:i + 1], func=AF.Ln,
                           scale=1.0 / E, bias=epsb[:, 0:1]).then_inc(act.h, 1)
            eng.wait_ge(act.h, t_lnv)
            eng.activation(out=st_rstd[:, i:i + 1], in_=st_lnv[:, i:i + 1], func=AF.Exp,
                           scale=-0.5).then_inc(act.h, 1)

        t_apply = dve.tok(1)

        @OP("vector")
        def _(eng, i=i, src_tile=src_tile, t_rstd=t_rstd, war=last["xntm"]):
            eng.wait_ge(act.h, t_rstd)
            W(eng, war)
            eng.tensor_scalar(out=xntm[:], in0=src_tile[:], scalar1=st_negmean[:, i:i + 1],
                              scalar2=st_rstd[:, i:i + 1], op0=ALU.add,
                              op1=ALU.mult).then_inc(dve.h, 1)

        for b in range(2):
            t_tr = pe.tok(1)
            last["xntm"] = (pe.h, t_tr)

            @OP("tensor")
            def _(eng, b=b, t_apply=t_apply, t_misc=t_misc, war=last["ps_t"]):
                eng.wait_ge(dve.h, t_apply)
                eng.wait_ge(sems["s_misc"].h, t_misc)
                W(eng, war)
                for j in range(8):
                    e_ch = b * 8 + j
                    mm = eng.transpose(ps_t[:, j * 128:(j + 1) * 128],
                                       xntm[:, e_ch * 128:(e_ch + 1) * 128], ident_b)
                mm.then_inc(pe.h, 1)

            if b == 0:
                t_ev = act.tok(1)
                last["ps_t"] = (act.h, t_ev)

                @OP("scalar")
                def _(eng, b=b, t_tr=t_tr, dst_view=dst_view, dst_col0=dst_col0,
                      dst_war=dst_war):
                    eng.wait_ge(pe.h, t_tr)
                    W(eng, dst_war)
                    dst = dst_view[:, b * 8:(b + 1) * 8, dst_col0:dst_col0 + 128]
                    src = ps_t[:].rearrange("p (e n) -> p e n", e=8)
                    eng.copy(out=dst, in_=src).then_inc(act.h, 1)
            else:
                t_ev = dve.tok(1)
                last["ps_t"] = (dve.h, t_ev)

                @OP("vector")
                def _(eng, b=b, t_tr=t_tr, dst_view=dst_view, dst_col0=dst_col0,
                      dst_war=dst_war):
                    eng.wait_ge(pe.h, t_tr)
                    W(eng, dst_war)
                    dst = dst_view[:, b * 8:(b + 1) * 8, dst_col0:dst_col0 + 128]
                    src = ps_t[:].rearrange("p (e n) -> p e n", e=8)
                    eng.tensor_copy(out=dst, in_=src).then_inc(dve.h, 1)
        return t_apply

    psb_rot = [0]

    def next_psb():
        s_ = psb_rot[0]
        psb_rot[0] ^= 1
        return s_

    prod_q_tok = {}
    prod_k_tok = {}
    prod_v_tok = {}
    qkv_ready = {}
    xn_ready_holder = [0]

    def emit_weight_loads(h):
        sems["s_qw"].tok(16)
        sems["s_kw"].tok(16)
        sems["s_vw"].tok(16)

        @OP("gpsimd")
        def _(eng, h=h, wq=last["wq"], wk=last["wk"], wv=last["wv"]):
            W(eng, wq)
            eng.dma_start(out=wqb[:].rearrange("p (e n) -> p e n", e=16),
                          in_=qw_d[h]).then_inc(sems["s_qw"].h, 16)
            W(eng, wk)
            eng.dma_start(out=wkb[:].rearrange("p (e n) -> p e n", e=16),
                          in_=kw_d[h]).then_inc(sems["s_kw"].h, 16)
            W(eng, wv)
            eng.dma_start(out=wvb[:].rearrange("p (e n) -> p e n", e=16),
                          in_=vw_d[h]).then_inc(sems["s_vw"].h, 16)

    def make_production(h, xn_tok_fn, interleave=False):
        qsl = h % 2
        units = []
        rd = qkv_ready.setdefault(h, {})
        stt = {"v_ev": None}

        def proj_unit(kind, nb):
            def emit():
                dstmap = {"q": Qb[qsl], "k": Kb[qsl], "v": Vfb}
                wmap = {"q": (wqb, sems["s_qw"]), "k": (wkb, sems["s_kw"]),
                        "v": (wvb, sems["s_vw"])}
                wsb, wsem = wmap[kind]
                wtok = 16 * (h + 1)
                slot = next_psb()
                t_g = pe.tok(1)

                @OP("tensor")
                def _(eng, nb=nb, slot=slot, wsb=wsb, wsem=wsem, wtok=wtok,
                      war=last[f"ps_b{slot}"], xr=xn_tok_fn(nb)):
                    eng.wait_ge(wsem.h, wtok)
                    eng.wait_ge(act.h, xr)
                    W(eng, war)
                    for k in range(16):
                        mm = eng.matmul(ps_b[slot][:],
                                        wsb[:, k * 128:(k + 1) * 128],
                                        bigA_bf[:, k * 2048 + nb * 512: k * 2048 + nb * 512 + 512],
                                        start=(k == 0), stop=(k == 15))
                    mm.then_inc(pe.h, 1)
                if nb == 3:
                    {"q": prod_q_tok, "k": prod_k_tok, "v": prod_v_tok}[kind][h] = t_g

                t_e = act.tok(1)
                hnd = act.h
                scl = SCALE if kind == "q" else None

                @OP("scalar")
                def _(eng, nb=nb, slot=slot, t_g=t_g, dst=dstmap[kind],
                      scl=scl):
                    eng.wait_ge(pe.h, t_g)
                    if scl is None:
                        eng.copy(out=dst[:, nb * 512:(nb + 1) * 512],
                                 in_=ps_b[slot][:]).then_inc(act.h, 1)
                    else:
                        eng.activation(out=dst[:, nb * 512:(nb + 1) * 512],
                                       in_=ps_b[slot][:], func=AF.Copy,
                                       scale=scl).then_inc(act.h, 1)
                last[f"ps_b{slot}"] = (hnd, t_e)
                if nb == 3:
                    rd[kind] = (hnd, t_e)
                if kind == "v":
                    stt["v_ev"] = (hnd, t_e)
            return emit

        def vt_unit(b):
            def emit():
                t_tr = pe.tok(1)

                @OP("tensor")
                def _(eng, b=b, v_ev=stt["v_ev"], war=last["ps_t"], t_misc=t_misc):
                    W(eng, v_ev, war)
                    eng.wait_ge(sems["s_misc"].h, t_misc)
                    for j in range(8):
                        blk = b * 8 + j
                        mm = eng.transpose(ps_t[:, j * 128:(j + 1) * 128],
                                           Vfb[:, blk * 128:(blk + 1) * 128], ident_b)
                    mm.then_inc(pe.h, 1)

                t_ev = act.tok(1)
                last["ps_t"] = (act.h, t_ev)

                @OP("scalar")
                def _(eng, b=b, t_tr=t_tr, qsl=qsl):
                    eng.wait_ge(pe.h, t_tr)
                    dst = Vtb[qsl][:].rearrange("p (c n) -> p c n", c=16)[:, b * 8:(b + 1) * 8, :]
                    src = ps_t[:].rearrange("p (c n) -> p c n", c=8)
                    eng.copy(out=dst, in_=src).then_inc(act.h, 1)
                if b == 1:
                    rd["vt"] = (act.h, t_ev)
            return emit

        if interleave:
            for nb in range(4):
                units.append(proj_unit("q", nb))
                units.append(proj_unit("k", nb))
                units.append(proj_unit("v", nb))
        else:
            for nb in range(4):
                units.append(proj_unit("q", nb))
            for nb in range(4):
                units.append(proj_unit("k", nb))
            for nb in range(4):
                units.append(proj_unit("v", nb))
        units.append(vt_unit(0))
        units.append(vt_unit(1))
        return units

    emit_weight_loads(0)
    units0 = make_production(0, lambda nb: ln1_tok[4 * nb + 3], interleave=True)
    xbuf = [xt[0], xt[1], x1t[0], x1t[1]]
    xkey = ["xt0", "xt1", "x1t0", "x1t1"]
    for i in range(NT):
        sl = i % 4
        t_x = sems[f"s_xa{sl}"].tok(16)
        qname = ("sync", "scalar")[i % 2] if i < 4 else ("sync", "scalar", "gpsimd")[(i - 4) % 3]

        @OP(qname)
        def _(eng, i=i, sl=sl, war=last[xkey[sl]]):
            W(eng, war)
            eng.dma_start(out=xbuf[sl], in_=x_d[i * 128:(i + 1) * 128, :]).then_inc(sems[f"s_xa{sl}"].h, 16)

        t_apply = emit_ln_tile(i, xbuf[sl], t_x, sems[f"s_xa{sl}"].h, xn_view, i * 128)
        last[xkey[sl]] = (dve.h, t_apply)
        ln1_tok[i] = act.n
        if i % 4 == 3:
            for u in units0[3 * (i // 4):3 * (i // 4) + 3]:
                u()
    for u in units0[12:]:
        u()
    xn_ready = act.n
    xn_ready_holder[0] = xn_ready

    if debug:
        @OP("sync")
        def _(eng, n=xn_ready):
            eng.wait_ge(act.h, n)
            eng.dma_start(out=dbg["xn"], in_=bigA_bf).then_inc(sems["s_dbg"].h, 16)
        sems["s_dbg"].tok(16)

    attr_tok = {}
    for h in range(H):
        qsl = h % 2
        if h + 1 < H:
            last["wq"] = (pe.h, prod_q_tok[h])
            last["wk"] = (pe.h, prod_k_tok[h])
            last["wv"] = (pe.h, prod_v_tok[h])
            emit_weight_loads(h + 1)
            pending = make_production(h + 1, lambda nb: xn_ready_holder[0])
        else:
            pending = []
            pe_h14_done = pe.n
            t_ow = sems["s_ow"].tok(16 * 16)

            @OP("gpsimd")
            def _(eng, tok=prod_v_tok[H - 1]):
                eng.wait_ge(pe.h, tok)
                for hh in range(16):
                    eng.dma_start(out=bigA_bf[:, hh * 2048:(hh + 1) * 2048],
                                  in_=ow_d[hh * 128:(hh + 1) * 128, :]).then_inc(sems["s_ow"].h, 16)
        rd = qkv_ready[h]

        probs_b = (probs, probs2)
        diag_b = (diag, diag2)
        pend_prev = None
        rcp_tok = {}
        for t in range(NT + 1):
            S_units = []
            if t < NT:
                KL = 128 * (t + 1)
                nblk = (KL + 511) // 512
                jd = (t * 128) // 512
                par = t % 2
                war_psa = last["ps_a"]
                mx_toks = []

                def mk_score(j, w, t=t, jd=jd, qsl=qsl, war=war_psa,
                             q_ev=rd["q"], k_ev=rd["k"]):
                    def emit():
                        btok = pe.tok(1)

                        @OP("tensor")
                        def _(eng, j=j, w=w, t=t, jd=jd, qsl=qsl, war=war,
                              q_ev=q_ev, k_ev=k_ev):
                            if j == 0:
                                W(eng, q_ev, k_ev, war)
                                eng.wait_ge(sems["s_misc"].h, t_misc)
                            if j == jd:
                                eng.matmul(ps_a[:, t * 128:(t + 1) * 128],
                                           ident_b, mask_b, start=True, stop=False)
                                mm = eng.matmul(ps_a[:, j * 512:j * 512 + w],
                                                Qb[qsl][:, t * 128:(t + 1) * 128],
                                                Kb[qsl][:, j * 512:j * 512 + w],
                                                start=False, stop=True)
                            else:
                                mm = eng.matmul(ps_a[:, j * 512:j * 512 + w],
                                                Qb[qsl][:, t * 128:(t + 1) * 128],
                                                Kb[qsl][:, j * 512:j * 512 + w],
                                                start=True, stop=True)
                            mm.then_inc(pe.h, 1)

                        mx_toks.append(dve.tok(1))

                        @OP("vector")
                        def _(eng, j=j, w=w, btok=btok):
                            eng.wait_ge(pe.h, btok)
                            eng.reduce_max(out=st_pm[:, j:j + 1],
                                           in_=ps_a[:, j * 512:j * 512 + w],
                                           axis=AX.X).then_inc(dve.h, 1)
                    return emit

                for j in range(nblk):
                    S_units.append(mk_score(j, min(512, KL - j * 512)))

            U_units = []
            if pend_prev is not None:
                pt, ppar, p_exp_toks = pend_prev
                pprobs = probs_b[ppar]
                nbat = (pt + 4) // 4
                slc = pt % 4
                evt = []
                war_pv = last["ps_c"]
                vt_ev = rd["vt"]
                trk = {"t_tr": None, "t_pv": None}

                def mk_u(b, bw, pt=pt, slc=slc, qsl=qsl,
                         p_exp=p_exp_toks, pprobs=pprobs, war_pv=war_pv,
                         vt_ev=vt_ev, nbat=nbat):
                    def emit():
                        slot = next_psb()
                        t_tr = pe.tok(1)
                        trk["t_tr"] = t_tr

                        @OP("tensor")
                        def _(eng, b=b, bw=bw, slot=slot, p_exp=p_exp[b],
                              pprobs=pprobs, war=last[f"ps_b{slot}"]):
                            eng.wait_ge(act.h, p_exp)
                            W(eng, war)
                            for j in range(bw):
                                kb = b * 4 + j
                                mm = eng.matmul(ps_b[slot][:, j * 128:(j + 1) * 128],
                                                pprobs[:, kb * 128:(kb + 1) * 128],
                                                ident_b, start=True, stop=True)
                            mm.then_inc(pe.h, 1)

                        if b % 2 == 0:
                            t_evt = dve.tok(1)
                            last[f"ps_b{slot}"] = (dve.h, t_evt)
                            evt.append((dve.h, t_evt))

                            @OP("vector")
                            def _(eng, b=b, bw=bw, slot=slot, t_tr=t_tr,
                                  war=last["probsT"]):
                                eng.wait_ge(pe.h, t_tr)
                                W(eng, war)
                                eng.tensor_copy(out=probsT[:, b * 512:b * 512 + bw * 128],
                                                in_=ps_b[slot][:, 0:bw * 128]).then_inc(dve.h, 1)
                        else:
                            t_evt = act.tok(1)
                            last[f"ps_b{slot}"] = (act.h, t_evt)
                            evt.append((act.h, t_evt))

                            @OP("scalar")
                            def _(eng, b=b, bw=bw, slot=slot, t_tr=t_tr,
                                  war=last["probsT"]):
                                eng.wait_ge(pe.h, t_tr)
                                W(eng, war)
                                eng.copy(out=probsT[:, b * 512:b * 512 + bw * 128],
                                         in_=ps_b[slot][:, 0:bw * 128]).then_inc(act.h, 1)

                        if b >= 1:
                            pb = b - 1
                            t_pv = pe.tok(1)
                            trk["t_pv"] = t_pv

                            @OP("tensor")
                            def _(eng, pb=pb, slc=slc, qsl=qsl, ev=evt[pb],
                                  vt_ev=(vt_ev if pb == 0 else None),
                                  war=(war_pv if pb == 0 else None)):
                                W(eng, ev, vt_ev, war)
                                for kb in range(pb * 4, pb * 4 + 4):
                                    mm = eng.matmul(ps_c[:, slc * 128:(slc + 1) * 128],
                                                    Vtb[qsl][:, kb * 128:(kb + 1) * 128],
                                                    probsT[:, kb * 128:(kb + 1) * 128],
                                                    start=(kb == 0), stop=False)
                                mm.then_inc(pe.h, 1)
                    return emit

                for b in range(nbat):
                    U_units.append(mk_u(b, min(4, (pt + 1) - b * 4)))

                def mk_tail(pt=pt, ppar=ppar, slc=slc, qsl=qsl, nbat=nbat,
                            war_pv=war_pv, vt_ev=vt_ev, h=h):
                    def emit():
                        fb = nbat - 1
                        fw = (pt + 1) - fb * 4
                        t_pv = pe.tok(1)

                        @OP("tensor")
                        def _(eng, pt=pt, fb=fb, fw=fw, slc=slc, qsl=qsl,
                              ev=evt[fb], vt_ev=(vt_ev if fb == 0 else None),
                              war=(war_pv if fb == 0 else None)):
                            W(eng, ev, vt_ev, war)
                            for kb in range(fb * 4, fb * 4 + fw):
                                mm = eng.matmul(ps_c[:, slc * 128:(slc + 1) * 128],
                                                Vtb[qsl][:, kb * 128:(kb + 1) * 128],
                                                probsT[:, kb * 128:(kb + 1) * 128],
                                                start=(kb == 0), stop=(kb == pt))
                            mm.then_inc(pe.h, 1)

                        last[f"probs{ppar}"] = (pe.h, trk["t_tr"])
                        last["probsT"] = (pe.h, t_pv)

                        if pt % 4 == 3:
                            g4 = pt // 4
                            prev_mm2 = None
                            for i in range(4):
                                tile = g4 * 4 + i
                                slot = next_psb()
                                t_m1 = pe.tok(1)

                                @OP("tensor")
                                def _(eng, tile=tile, slot=slot,
                                      rtok=rcp_tok[pt], war=last[f"ps_b{slot}"],
                                      t_misc=t_misc):
                                    eng.wait_ge(dve.h, rtok)
                                    eng.wait_ge(sems["s_misc"].h, t_misc)
                                    W(eng, war)
                                    mm = eng.matmul(ps_b[slot][0:1, 0:128],
                                                    st_rcp[:, tile:tile + 1],
                                                    ident_f, start=True, stop=True)
                                    mm.then_inc(pe.h, 1)

                                t_rev = act.tok(1)

                                @OP("scalar")
                                def _(eng, slot=slot, t_m1=t_m1,
                                      prev_mm2=prev_mm2):
                                    eng.wait_ge(pe.h, t_m1)
                                    W(eng, prev_mm2)
                                    eng.copy(out=rbT1[0:1, :],
                                             in_=ps_b[slot][0:1, 0:128]).then_inc(act.h, 1)

                                t_m2 = pe.tok(1)
                                prev_mm2 = (pe.h, t_m2)

                                @OP("tensor")
                                def _(eng, slot=slot, t_rev=t_rev):
                                    eng.wait_ge(act.h, t_rev)
                                    mm = eng.matmul(ps_b[slot][:, 128:256],
                                                    ones1[0:1, :], rbT1[0:1, :],
                                                    start=True, stop=True)
                                    mm.then_inc(pe.h, 1)

                                t_bev = dve.tok(1)
                                last[f"ps_b{slot}"] = (dve.h, t_bev)

                                @OP("vector")
                                def _(eng, i=i, slot=slot, t_m2=t_m2,
                                      war=last["rbB"]):
                                    eng.wait_ge(pe.h, t_m2)
                                    W(eng, war)
                                    eng.tensor_copy(out=rbB[:, i * 128:(i + 1) * 128],
                                                    in_=ps_b[slot][:, 128:256]).then_inc(dve.h, 1)
                                bev_last = t_bev

                            t_pvev = dve.tok(1)
                            last["ps_c"] = (dve.h, t_pvev)
                            last["rbB"] = (dve.h, t_pvev)

                            @OP("vector")
                            def _(eng, t_pv=t_pv, bev=bev_last, war=last["atst"]):
                                eng.wait_ge(pe.h, t_pv)
                                eng.wait_ge(dve.h, bev)
                                W(eng, war)
                                eng.tensor_tensor(out=atst[:], in0=ps_c[:],
                                                  in1=rbB[:],
                                                  op=ALU.mult).then_inc(dve.h, 1)

                            t_w = sems["s_attw"].tok(64)
                            last["atst"] = (sems["s_attw"].h, t_w)

                            @OP("sync")
                            def _(eng, h=h, g4=g4, t_pvev=t_pvev):
                                eng.wait_ge(dve.h, t_pvev)
                                for i in range(4):
                                    tt = g4 * 4 + i
                                    eng.dma_start(out=attn_scr[tt][:, h, :],
                                                  in_=atst[:, i * 128:(i + 1) * 128]).then_inc(sems["s_attw"].h, 16)

                            if h == H - 1 and g4 == 0:
                                for tp in range(2):
                                    attr_tok[tp] = sems[f"s_attr{tp}"].tok(16)

                                    @OP("sync")
                                    def _(eng, tp=tp, aw=sems["s_attw"].n,
                                          ph=pe_h14_done):
                                        eng.wait_ge(sems["s_attw"].h, aw)
                                        eng.wait_ge(pe.h, ph)
                                        eng.dma_start(out=attC[tp][:].rearrange("p (c n) -> p c n", c=16),
                                                      in_=attn_scr[tp]).then_inc(sems[f"s_attr{tp}"].h, 16)
                    return emit

                U_units.append(mk_tail())

            order = U_units[:1]
            si, ui = 0, 1
            while si < len(S_units) or ui < len(U_units):
                if si < len(S_units):
                    order.append(S_units[si])
                    si += 1
                if ui < len(U_units):
                    order.append(U_units[ui])
                    ui += 1
            for u in order:
                u()

            if t < NT:
                t_nb = dve.tok(1)

                @OP("vector")
                def _(eng, t=t, nblk=nblk, t_last=mx_toks[-1]):
                    eng.wait_ge(dve.h, t_last)
                    eng.reduce_max(out=st_negb[:, t:t + 1], in_=st_pm[:, 0:nblk],
                                   axis=AX.X, negate=True).then_inc(dve.h, 1)

                nbt = (t + 4) // 4
                exp_toks = []
                for b in range(nbt):
                    w = min(512, KL - b * 512)
                    exp_toks.append(act.tok(1))

                    @OP("scalar")
                    def _(eng, t=t, b=b, w=w, par=par, t_nb=t_nb,
                          war=(last[f"probs{par}"] if b == 0 else None),
                          war_se=(last["st_se"] if b == 0 else None)):
                        if b == 0:
                            eng.wait_ge(dve.h, t_nb)
                            W(eng, war, war_se)
                        eng.activation(out=probs_b[par][:, b * 512:b * 512 + w],
                                       in_=ps_a[:, b * 512:b * 512 + w],
                                       func=AF.Exp, bias=st_negb[:, t:t + 1],
                                       accum_out=st_se[:, b:b + 1]).then_inc(act.h, 1)

                last["ps_a"] = (act.h, exp_toks[-1])

                if pending:
                    pending.pop(0)()

                t_sum = dve.tok(1)
                t_rcp = dve.tok(1)
                last["st_se"] = (dve.h, t_sum)
                rcp_tok[t] = t_rcp

                @OP("vector")
                def _(eng, t=t, nbt=nbt, t_le=exp_toks[-1], t_sum=t_sum):
                    eng.wait_ge(act.h, t_le)
                    eng.reduce_sum(out=st_ssum[:, t:t + 1], in_=st_se[:, 0:nbt],
                                   axis=AX.X).then_inc(dve.h, 1)
                    eng.wait_ge(dve.h, t_sum)
                    eng.reciprocal(out=st_rcp[:, t:t + 1],
                                   in_=st_ssum[:, t:t + 1]).then_inc(dve.h, 1)

                pend_prev = (t, par, tuple(exp_toks))
        for u in pending:
            u()

    barrier(engines=("tensor", "vector", "scalar"))
    xn2_view = xn2h[:].rearrange("p (e n) -> p e n", e=16)
    ln2_tok = {}
    ln2c_apply = {}
    if debug:
        @OP("sync")
        def _(eng, aw=sems["s_attw"].n):
            eng.wait_ge(sems["s_attw"].h, aw)
            eng.dma_start(out=dbg["at"], in_=attn_scr).then_inc(sems["s_dbg"].h, 16)
        sems["s_dbg"].tok(16)

    for t in range(NT):
        sl = t % 2
        t_xc = sems[f"s_xc{sl}"].tok(16)
        if t < 2:
            t_ar = attr_tok[t]
        else:
            t_ar = sems[f"s_attr{sl}"].tok(16)

            @OP("sync")
            def _(eng, t=t, sl=sl, war_a=last[f"attC{sl}"]):
                W(eng, war_a)
                eng.dma_start(out=attC[sl][:].rearrange("p (c n) -> p c n", c=16),
                              in_=attn_scr[t]).then_inc(sems[f"s_attr{sl}"].h, 16)

        @OP("scalar")
        def _(eng, t=t, sl=sl, war_x=last[f"xt{sl}"]):
            W(eng, war_x)
            eng.dma_start(out=xt[sl], in_=x_d[t * 128:(t + 1) * 128, :]).then_inc(sems[f"s_xc{sl}"].h, 16)

        t_g_last = None
        t_ev_last = None
        for nb in range(4):
            slot = next_psb()
            t_g = pe.tok(1)
            t_g_last = t_g

            @OP("tensor")
            def _(eng, t=t, nb=nb, sl=sl, slot=slot, t_ar=t_ar, t_ow=t_ow,
                  war=last[f"ps_b{slot}"]):
                eng.wait_ge(sems["s_ow"].h, t_ow)
                eng.wait_ge(sems[f"s_attr{sl}"].h, t_ar)
                W(eng, war)
                for hh in range(16):
                    mm = eng.matmul(ps_b[slot][:],
                                    attC[sl][:, hh * 128:(hh + 1) * 128],
                                    bigA_bf[:, hh * 2048 + nb * 512: hh * 2048 + nb * 512 + 512],
                                    start=(hh == 0), stop=(hh == 15))
                mm.then_inc(pe.h, 1)

            t_ev = dve.tok(1)
            last[f"ps_b{slot}"] = (dve.h, t_ev)
            t_ev_last = t_ev

            @OP("vector")
            def _(eng, t=t, nb=nb, sl=sl, slot=slot, t_g=t_g, t_xc=t_xc,
                  war=last[f"x1t{sl}"]):
                eng.wait_ge(pe.h, t_g)
                eng.wait_ge(sems[f"s_xc{sl}"].h, t_xc)
                W(eng, war)
                eng.tensor_tensor(out=x1t[sl][:, nb * 512:(nb + 1) * 512],
                                  in0=ps_b[slot][:], in1=xt[sl][:, nb * 512:(nb + 1) * 512],
                                  op=ALU.add).then_inc(dve.h, 1)

        last[f"attC{sl}"] = (pe.h, t_g_last)
        last[f"xt{sl}"] = (dve.h, t_ev_last)
        t_x1 = sems[f"s_x1w{sl}"].tok(16)
        last[f"x1t{sl}"] = (sems[f"s_x1w{sl}"].h, t_x1)

        @OP("gpsimd")
        def _(eng, t=t, sl=sl, t_ev=t_ev_last):
            eng.wait_ge(dve.h, t_ev)
            eng.dma_start(out=x1_scr[t * 128:(t + 1) * 128, :], in_=x1t[sl][:]).then_inc(sems[f"s_x1w{sl}"].h, 16)

        if t < 8:
            t_ap2 = emit_ln_tile(t, x1t[sl], t_ev_last, dve.h, xn2_view,
                                 t * 128)
            ln2c_apply[sl] = (dve.h, t_ap2)
    ln2_tok[0] = act.n

    if debug:
        @OP("sync")
        def _(eng, n0=sems["s_x1w0"].n, n1=sems["s_x1w1"].n):
            eng.wait_ge(sems["s_x1w0"].h, n0)
            eng.wait_ge(sems["s_x1w1"].h, n1)
            eng.dma_start(out=dbg["x1"], in_=x1_scr).then_inc(sems["s_dbg"].h, 16)
        sems["s_dbg"].tok(16)

    barrier(extra=(sems["s_x1w0"], sems["s_x1w1"]))

    y_toks = {}
    for hf in range(2):
        toff = hf * 8

        dai_tok = {}
        for i in range(8):
            tt = toff + i
            dai_tok[i] = sems["s_dai"].tok(16)

            @OP("sync")
            def _(eng, tt=tt, i=i, war_y=(y_toks.get(i) if hf else None)):
                if war_y:
                    eng.wait_ge(sems["s_y"].h, war_y)
                eng.dma_start(out=bigA[:, i * 2048:(i + 1) * 2048],
                              in_=x1_scr[tt * 128:(tt + 1) * 128, :]).then_inc(sems["s_dai"].h, 16)

        ln2_act = ln2_tok[hf]

        for g in range(NGRP):
            t_mul_last = None
            for hcg in range(HGRP):
                hc = g * HGRP + hcg
                wsl = hc % 2
                t_gww = sems[f"s_gw{wsl}"].tok(16)
                t_uww = sems[f"s_uw{wsl}"].tok(16)
                sems["s_dw"].tok(16)

                @OP("gpsimd")
                def _(eng, hc=hc, hcg=hcg, wsl=wsl, war_g=last[f"gw{wsl}"],
                      war_u=last[f"uw{wsl}"], war_d=last["dwg"] if hcg == 0 else None):
                    W(eng, war_g)
                    eng.dma_start(out=gwb[wsl][:].rearrange("p (e n) -> p e n", e=16),
                                  in_=gw_d[hc]).then_inc(sems[f"s_gw{wsl}"].h, 16)
                    W(eng, war_u)
                    eng.dma_start(out=uwb[wsl][:].rearrange("p (e n) -> p e n", e=16),
                                  in_=uw_d[hc]).then_inc(sems[f"s_uw{wsl}"].h, 16)
                    if war_d:
                        W(eng, war_d)
                    eng.dma_start(out=dwg[:, hcg * 2048:(hcg + 1) * 2048],
                                  in_=dw_d[hc]).then_inc(sems["s_dw"].h, 16)

                gts, uts = [], []
                for ch in range(2):
                    go = ch * 1024
                    uo = ch * 1024 + 512
                    t_gate = pe.tok(1)
                    gts.append(t_gate)

                    @OP("tensor")
                    def _(eng, ch=ch, wsl=wsl, go=go, t_w=t_gww,
                          ln2_act=ln2_act, war=last[f"ps_f{ch}"]):
                        eng.wait_ge(sems[f"s_gw{wsl}"].h, t_w)
                        eng.wait_ge(act.h, ln2_act)
                        W(eng, war)
                        for k in range(16):
                            mm = eng.matmul(ps_a[:, go:go + 512],
                                            gwb[wsl][:, k * 128:(k + 1) * 128],
                                            xn2h[:, k * 1024 + ch * 512: k * 1024 + ch * 512 + 512],
                                            start=(k == 0), stop=(k == 15))
                        mm.then_inc(pe.h, 1)

                    t_up = pe.tok(1)
                    uts.append(t_up)

                    @OP("tensor")
                    def _(eng, ch=ch, wsl=wsl, uo=uo, t_w=t_uww,
                          war=last[f"ps_f{ch}"]):
                        eng.wait_ge(sems[f"s_uw{wsl}"].h, t_w)
                        W(eng, war)
                        for k in range(16):
                            mm = eng.matmul(ps_a[:, uo:uo + 512],
                                            uwb[wsl][:, k * 128:(k + 1) * 128],
                                            xn2h[:, k * 1024 + ch * 512: k * 1024 + ch * 512 + 512],
                                            start=(k == 0), stop=(k == 15))
                        mm.then_inc(pe.h, 1)

                    t_sil = act.tok(1)

                    @OP("scalar")
                    def _(eng, ch=ch, go=go, t_gate=t_gate, war=last[f"sil{ch}"]):
                        eng.wait_ge(pe.h, t_gate)
                        W(eng, war)
                        eng.activation(out=sil[:, ch * 512:(ch + 1) * 512],
                                       in_=ps_a[:, go:go + 512],
                                       func=AF.Silu).then_inc(act.h, 1)

                    t_mul = dve.tok(1)
                    t_mul_last = t_mul
                    last[f"sil{ch}"] = (dve.h, t_mul)
                    last[f"ps_f{ch}"] = (dve.h, t_mul)

                    @OP("vector")
                    def _(eng, hcg=hcg, ch=ch, uo=uo, t_up=t_up, t_sil=t_sil,
                          war=last["hid"] if (hcg == 0 and ch == 0) else None):
                        eng.wait_ge(pe.h, t_up)
                        eng.wait_ge(act.h, t_sil)
                        if war:
                            W(eng, war)
                        eng.tensor_tensor(out=hid[:, hcg * 1024 + ch * 512: hcg * 1024 + ch * 512 + 512],
                                          in0=sil[:, ch * 512:(ch + 1) * 512],
                                          in1=ps_a[:, uo:uo + 512],
                                          op=ALU.mult).then_inc(dve.h, 1)

                last[f"gw{wsl}"] = (pe.h, gts[1])
                last[f"uw{wsl}"] = (pe.h, uts[1])
                last_up_tok = uts[1]

            t_dw_w = sems["s_dw"].n
            t_dn_last = None
            t_add_by = {}
            for ti in range(8):
                for nb in range(4):
                    slot = next_psb()
                    t_dn = pe.tok(1)
                    t_dn_last = t_dn

                    @OP("tensor")
                    def _(eng, ti=ti, nb=nb, slot=slot, t_dw_w=t_dw_w,
                          t_mul=t_mul_last, war=last[f"ps_b{slot}"]):
                        eng.wait_ge(sems["s_dw"].h, t_dw_w)
                        eng.wait_ge(dve.h, t_mul)
                        W(eng, war)
                        for hcg in range(HGRP):
                            mm = eng.matmul(ps_b[slot][:],
                                            hid[:, hcg * 1024 + ti * 128: hcg * 1024 + ti * 128 + 128],
                                            dwg[:, hcg * 2048 + nb * 512: hcg * 2048 + nb * 512 + 512],
                                            start=(hcg == 0), stop=(hcg == HGRP - 1))
                        mm.then_inc(pe.h, 1)

                    t_add = dve.tok(1)
                    last[f"ps_b{slot}"] = (dve.h, t_add)
                    t_add_by[ti] = t_add

                    @OP("vector")
                    def _(eng, ti=ti, nb=nb, slot=slot, t_dn=t_dn, t_dai=dai_tok[ti]):
                        eng.wait_ge(pe.h, t_dn)
                        eng.wait_ge(sems["s_dai"].h, t_dai)
                        dst = bigA[:, ti * 2048 + nb * 512: ti * 2048 + (nb + 1) * 512]
                        eng.tensor_tensor(out=dst, in0=dst, in1=ps_b[slot][:],
                                          op=ALU.add).then_inc(dve.h, 1)

                if hf == 0 and g == NGRP - 1:
                    tt2 = 8 + ti
                    sl2 = ti % 2
                    t_x1r = sems[f"s_x1r{sl2}"].tok(16)

                    @OP("sync")
                    def _(eng, tt2=tt2, sl2=sl2, war=last[f"x1t{sl2}"],
                          war2=ln2c_apply.get(sl2)):
                        W(eng, war, war2)
                        eng.dma_start(out=x1t[sl2], in_=x1_scr[tt2 * 128:(tt2 + 1) * 128, :]).then_inc(sems[f"s_x1r{sl2}"].h, 16)

                    t_ap = emit_ln_tile(ti, x1t[sl2], t_x1r,
                                        sems[f"s_x1r{sl2}"].h, xn2_view,
                                        ti * 128, dst_war=(pe.h, last_up_tok))
                    last[f"x1t{sl2}"] = (dve.h, t_ap)

            last["hid"] = (pe.h, t_dn_last)
            last["dwg"] = (pe.h, t_dn_last)
            if g == NGRP - 1:
                final_adds = dict(t_add_by)
        if hf == 0:
            ln2_tok[1] = act.n

        for ti in range(8):
            tt = toff + ti
            y_toks[ti] = sems["s_y"].tok(16)

            @OP("sync")
            def _(eng, tt=tt, ti=ti, t_add=final_adds[ti]):
                eng.wait_ge(dve.h, t_add)
                eng.dma_start(out=y_d[tt * 128:(tt + 1) * 128, :],
                              in_=bigA[:, ti * 2048:(ti + 1) * 2048]).then_inc(sems["s_y"].h, 16)

    @OP("sync")
    def _(eng, n=sems["s_y"].n, nd=sems["s_dbg"].n):
        eng.wait_ge(sems["s_y"].h, n)
        if nd:
            eng.wait_ge(sems["s_dbg"].h, nd)

    with stack:
        with nc.Block() as block:
            def runner(ename):
                def f(eng):
                    for fn in steps[ename]:
                        fn(eng)
                return f
            block.sync(runner("sync"))
            block.tensor(runner("tensor"))
            block.vector(runner("vector"))
            block.scalar(runner("scalar"))
            block.gpsimd(runner("gpsimd"))
    return nc


def _prep_inputs(x, q_wei, k_wei, v_wei, out_wei, w_gate, w_up, w_down):
    asc = np.ascontiguousarray
    qw = asc(np.asarray(q_wei, np.float32).reshape(16, 128, 16, 128).transpose(2, 1, 0, 3))
    kw = asc(np.asarray(k_wei, np.float32).reshape(16, 128, 16, 128).transpose(2, 1, 0, 3))
    vw = asc(np.asarray(v_wei, np.float32).reshape(16, 128, 16, 128).transpose(2, 1, 0, 3))
    ow = asc(np.asarray(out_wei, np.float32))
    gw = asc(np.asarray(w_gate, np.float32).reshape(64, 128, 16, 128).transpose(0, 3, 2, 1))
    uw = asc(np.asarray(w_up, np.float32).reshape(64, 128, 16, 128).transpose(0, 3, 2, 1))
    dw = asc(np.asarray(w_down, np.float32).T.reshape(64, 128, 2048))
    mask = np.triu(np.full((128, 128), -66504.0, dtype=np.float32), k=1)
    ident = np.eye(128, dtype=np.float32)
    x = np.asarray(x, np.float32)
    maps = []
    for c in range(8):
        maps.append(dict(x=asc(x[c]), qw=qw, kw=kw, vw=vw, ow=ow,
                         gw=gw, uw=uw, dw=dw, mask=mask, ident=ident))
    return maps


def kernel(L0_attn_hidden_states, L0_attn_q_wei, L0_attn_k_wei, L0_attn_v_wei,
           L0_attn_out_wei, L0_ffn_w_gate, L0_ffn_w_up, L0_ffn_w_down):
    maps = _prep_inputs(L0_attn_hidden_states, L0_attn_q_wei, L0_attn_k_wei,
                        L0_attn_v_wei, L0_attn_out_wei, L0_ffn_w_gate,
                        L0_ffn_w_up, L0_ffn_w_down)
    nc = build(debug=False)
    res = run_bass_kernel_spmd(nc, maps, core_ids=list(range(8)))
    return np.stack([np.asarray(r["y"]) for r in res.results], axis=0)


# version 13
# speedup vs baseline: 1.0841x; 1.0841x over previous
import math
import numpy as np
import concourse.bass as bass
import concourse.mybir as mybir
from concourse.bass_utils import run_bass_kernel_spmd

F32 = mybir.dt.float32
BF16 = mybir.dt.bfloat16
AF = mybir.ActivationFunctionType
ALU = mybir.AluOpType
AX = mybir.AxisListType

S = 2048
E = 2048
H = 16
D = 128
NT = 16
NHC = 64
SCALE = float(1.0 / math.sqrt(float(D)))
EPS = 1e-5
HGRP = 8
NGRP = NHC // HGRP

ENGINES = ("sync", "tensor", "vector", "scalar", "gpsimd")


class SemT:
    def __init__(self, h):
        self.h = h
        self.n = 0

    def tok(self, k):
        self.n += k
        return self.n


def build(debug=False):
    nc = bass.Bass("TRN2", target_bir_lowering=False)

    x_d = nc.dram_tensor("x", [S, E], F32, kind="ExternalInput").ap()
    qw_d = nc.dram_tensor("qw", [H, 128, 16, 128], F32, kind="ExternalInput").ap()
    kw_d = nc.dram_tensor("kw", [H, 128, 16, 128], F32, kind="ExternalInput").ap()
    vw_d = nc.dram_tensor("vw", [H, 128, 16, 128], F32, kind="ExternalInput").ap()
    ow_d = nc.dram_tensor("ow", [E, E], F32, kind="ExternalInput").ap()
    gw_d = nc.dram_tensor("gw", [NHC, 128, 16, 128], F32, kind="ExternalInput").ap()
    uw_d = nc.dram_tensor("uw", [NHC, 128, 16, 128], F32, kind="ExternalInput").ap()
    dw_d = nc.dram_tensor("dw", [NHC, 128, E], F32, kind="ExternalInput").ap()
    mask_d = nc.dram_tensor("mask", [128, 128], F32, kind="ExternalInput").ap()
    ident_d = nc.dram_tensor("ident", [128, 128], F32, kind="ExternalInput").ap()
    y_d = nc.dram_tensor("y", [S, E], F32, kind="ExternalOutput").ap()
    attn_scr = nc.dram_tensor("attn_scr", [NT, 128, H, 128], BF16).ap()
    x1_scr = nc.dram_tensor("x1_scr", [S, E], F32).ap()
    dbg = {}
    if debug:
        dbg["xn"] = nc.dram_tensor("d_xn", [128, 32768], BF16, kind="ExternalOutput").ap()
        dbg["at"] = nc.dram_tensor("d_at", [NT, 128, H, 128], BF16, kind="ExternalOutput").ap()
        dbg["x1"] = nc.dram_tensor("d_x1", [S, E], F32, kind="ExternalOutput").ap()
        dbg["xn2"] = nc.dram_tensor("d_xn2", [128, 16384], BF16, kind="ExternalOutput").ap()

    MAP_BYTES = 208896
    guard = nc.alloc_sbuf_tensor("manual_region", [128, MAP_BYTES], mybir.dt.uint8)
    SB_BASE = nc.lookup_mloc(guard).addr

    def sb(name, shape, dtype, off):
        nb = int(np.prod(shape[1:])) * (2 if dtype == BF16 else 4)
        assert off + nb <= MAP_BYTES, (name, off, nb)
        return nc.alloc_sbuf_tensor_at(name, shape, dtype, offset=SB_BASE + off).ap()

    bigA = sb("bigA", [128, 16384], F32, 0)
    bigA_bf = bigA.bitcast(BF16)
    Qb = [sb(f"Qb{i}", [128, 2048], BF16, 65536 + i * 4096) for i in range(2)]
    Kb = [sb(f"Kb{i}", [128, 2048], BF16, 73728 + i * 4096) for i in range(2)]
    Vtb = [sb(f"Vtb{i}", [128, 2048], BF16, 81920 + i * 4096) for i in range(2)]
    Vfb = sb("Vfb", [128, 2048], BF16, 90112)
    probs = sb("probs", [128, 2048], BF16, 94208)
    probsT = sb("probsT", [128, 2048], BF16, 98304)
    diag = sb("diag", [128, 128], BF16, 102400)
    atst = sb("atst", [128, 512], BF16, 102656)
    attC = [sb(f"attC{i}", [128, 2048], BF16, 114688 + i * 4096) for i in range(2)]
    xn2h = sb("xn2h", [128, 16384], BF16, 65536)
    ident_b = sb("ident_b", [128, 128], BF16, 103680)
    mask_b = sb("mask_b", [128, 128], BF16, 103936)
    st_negmean = sb("st_negmean", [128, 16], F32, 104192)
    st_ssq = sb("st_ssq", [128, 16], F32, 104256)
    st_lnv = sb("st_lnv", [128, 16], F32, 104320)
    st_rstd = sb("st_rstd", [128, 16], F32, 104384)
    st_negmax = sb("st_negmax", [128, 16], F32, 104448)
    st_negb = sb("st_negb", [128, 16], F32, 104512)
    st_ssum = sb("st_ssum", [128, 16], F32, 104576)
    st_lns = sb("st_lns", [128, 16], F32, 104640)
    st_rcp = sb("st_rcp", [128, 16], F32, 104704)
    epsb = sb("epsb", [128, 1], F32, 104768)
    st_pm = sb("st_pm", [128, 4], F32, 104832)
    probs2 = sb("probs2", [128, 2048], BF16, 104960)
    diag2 = sb("diag2", [128, 128], BF16, 109056)
    wqb = sb("wqb", [128, 2048], BF16, 114688)
    wkb = sb("wkb", [128, 2048], BF16, 118784)
    wvb = sb("wvb", [128, 2048], BF16, 122880)
    dwg = sb("dwg", [128, 16384], BF16, 114688)
    xt = [sb(f"xt{i}", [128, 2048], F32, 147456 + i * 8192) for i in range(2)]
    gwb = [sb(f"gwb{i}", [128, 2048], BF16, 147456 + i * 4096) for i in range(2)]
    uwb = [sb(f"uwb{i}", [128, 2048], BF16, 155648 + i * 4096) for i in range(2)]
    x1t = [sb(f"x1t{i}", [128, 2048], F32, 163840 + i * 8192) for i in range(2)]
    hid = sb("hid", [128, 8192], BF16, 180224)
    sq = sb("sq", [128, 2048], F32, 180224)
    xntm = sb("xntm", [128, 2048], BF16, 196608)
    sil = sb("sil", [128, 1024], F32, 200704)

    ps_a = nc.alloc_psum_tensor("ps_a", [128, 2048], F32).ap()
    ps_b = [nc.alloc_psum_tensor(f"ps_b{i}", [128, 512], F32).ap() for i in range(2)]
    ps_t = nc.alloc_psum_tensor("ps_t", [128, 1024], BF16).ap()
    ps_c = nc.alloc_psum_tensor("ps_c", [128, 512], F32).ap()

    import contextlib
    stack = contextlib.ExitStack()
    sems = {}
    for name in ("pe", "dve", "act", "s_misc", "s_xa0", "s_xa1", "s_xa2",
                 "s_xa3", "s_qw", "s_kw",
                 "s_vw", "s_attw", "s_ow", "s_attr0", "s_attr1", "s_xc0", "s_xc1",
                 "s_x1w0", "s_x1w1", "s_x1r0", "s_x1r1", "s_dai",
                 "s_gw0", "s_gw1", "s_uw0", "s_uw1", "s_dw", "s_y", "s_dbg"):
        sems[name] = SemT(stack.enter_context(nc.semaphore(name)))
    pe, dve, act = sems["pe"], sems["dve"], sems["act"]

    steps = {e: [] for e in ENGINES}

    def OP(e):
        def deco(fn):
            steps[e].append(fn)
        return deco

    def barrier(extra=(), engines=ENGINES):
        targets = [(s.h, s.n) for s in (pe, dve, act)] + [(s.h, s.n) for s in extra]
        targets = [(h, n) for h, n in targets if n > 0]
        for e in engines:
            def f(eng, targets=tuple(targets)):
                for h, n in targets:
                    eng.wait_ge(h, n)
            steps[e].append(f)

    last = {
        "ps_t": None, "ps_a": None, "ps_b0": None, "ps_b1": None, "ps_c": None,
        "ps_f0": None, "ps_f1": None, "sil0": None, "sil1": None,
        "probs0": None, "probs1": None, "probsT": None,
        "diag0": None, "diag1": None, "atst": None,
        "xntm": None, "sil": None, "hid": None, "dwg": None,
        "xt0": None, "xt1": None, "x1t0": None, "x1t1": None,
        "wq": None, "wk": None, "wv": None,
        "gw0": None, "gw1": None, "uw0": None, "uw1": None,
        "attC0": None, "attC1": None,
    }

    def W(eng, *keys):
        for k in keys:
            v = last[k] if isinstance(k, str) else k
            if v:
                eng.wait_ge(v[0], v[1])

    xn_view = bigA_bf.rearrange("p (e n) -> p e n", e=16)

    @OP("gpsimd")
    def _(eng):
        eng.dma_start(out=ident_b, in_=ident_d).then_inc(sems["s_misc"].h, 16)
        eng.dma_start(out=mask_b, in_=mask_d).then_inc(sems["s_misc"].h, 16)
    t_misc = sems["s_misc"].tok(32)

    t_eps = dve.tok(1)

    @OP("vector")
    def _(eng):
        eng.memset(epsb[:], EPS).then_inc(dve.h, 1)

    ln1_tok = {}

    def emit_ln_tile(i, src_tile, t_src, src_sem, dst_view, dst_col0,
                     dst_war=None):
        t_red = dve.tok(1)
        t_nm = dve.tok(1)

        @OP("vector")
        def _(eng, i=i, src_tile=src_tile, t_src=t_src, src_sem=src_sem,
              t_red=t_red):
            eng.wait_ge(src_sem, t_src)
            eng.reduce_sum(out=st_negmean[:, i:i + 1], in_=src_tile[:], axis=AX.X).then_inc(dve.h, 1)
            eng.wait_ge(dve.h, t_red)
            eng.tensor_scalar_mul(out=st_negmean[:, i:i + 1], in0=st_negmean[:, i:i + 1],
                                  scalar1=-1.0 / E).then_inc(dve.h, 1)

        t_ssq = act.tok(1)
        t_lnv = act.tok(1)
        t_rstd = act.tok(1)

        @OP("scalar")
        def _(eng, i=i, src_tile=src_tile, t_nm=t_nm, t_ssq=t_ssq,
              t_lnv=t_lnv, t_eps=t_eps, war=last["xntm"]):
            eng.wait_ge(dve.h, t_nm)
            W(eng, war)
            eng.activation(out=xntm[:], in_=src_tile[:], func=AF.Square,
                           bias=st_negmean[:, i:i + 1],
                           accum_out=st_ssq[:, i:i + 1]).then_inc(act.h, 1)
            eng.wait_ge(act.h, t_ssq)
            eng.wait_ge(dve.h, t_eps)
            eng.activation(out=st_lnv[:, i:i + 1], in_=st_ssq[:, i:i + 1], func=AF.Ln,
                           scale=1.0 / E, bias=epsb[:, 0:1]).then_inc(act.h, 1)
            eng.wait_ge(act.h, t_lnv)
            eng.activation(out=st_rstd[:, i:i + 1], in_=st_lnv[:, i:i + 1], func=AF.Exp,
                           scale=-0.5).then_inc(act.h, 1)

        t_apply = dve.tok(1)

        @OP("vector")
        def _(eng, i=i, src_tile=src_tile, t_rstd=t_rstd, war=last["xntm"]):
            eng.wait_ge(act.h, t_rstd)
            W(eng, war)
            eng.tensor_scalar(out=xntm[:], in0=src_tile[:], scalar1=st_negmean[:, i:i + 1],
                              scalar2=st_rstd[:, i:i + 1], op0=ALU.add,
                              op1=ALU.mult).then_inc(dve.h, 1)

        for b in range(2):
            t_tr = pe.tok(1)
            last["xntm"] = (pe.h, t_tr)

            @OP("tensor")
            def _(eng, b=b, t_apply=t_apply, t_misc=t_misc, war=last["ps_t"]):
                eng.wait_ge(dve.h, t_apply)
                eng.wait_ge(sems["s_misc"].h, t_misc)
                W(eng, war)
                for j in range(8):
                    e_ch = b * 8 + j
                    mm = eng.transpose(ps_t[:, j * 128:(j + 1) * 128],
                                       xntm[:, e_ch * 128:(e_ch + 1) * 128], ident_b)
                mm.then_inc(pe.h, 1)

            if b == 0:
                t_ev = act.tok(1)
                last["ps_t"] = (act.h, t_ev)

                @OP("scalar")
                def _(eng, b=b, t_tr=t_tr, dst_view=dst_view, dst_col0=dst_col0,
                      dst_war=dst_war):
                    eng.wait_ge(pe.h, t_tr)
                    W(eng, dst_war)
                    dst = dst_view[:, b * 8:(b + 1) * 8, dst_col0:dst_col0 + 128]
                    src = ps_t[:].rearrange("p (e n) -> p e n", e=8)
                    eng.copy(out=dst, in_=src).then_inc(act.h, 1)
            else:
                t_ev = dve.tok(1)
                last["ps_t"] = (dve.h, t_ev)

                @OP("vector")
                def _(eng, b=b, t_tr=t_tr, dst_view=dst_view, dst_col0=dst_col0,
                      dst_war=dst_war):
                    eng.wait_ge(pe.h, t_tr)
                    W(eng, dst_war)
                    dst = dst_view[:, b * 8:(b + 1) * 8, dst_col0:dst_col0 + 128]
                    src = ps_t[:].rearrange("p (e n) -> p e n", e=8)
                    eng.tensor_copy(out=dst, in_=src).then_inc(dve.h, 1)
        return t_apply

    psb_rot = [0]

    def next_psb():
        s_ = psb_rot[0]
        psb_rot[0] ^= 1
        return s_

    prod_q_tok = {}
    prod_k_tok = {}
    prod_v_tok = {}
    qkv_ready = {}
    xn_ready_holder = [0]

    def emit_weight_loads(h):
        sems["s_qw"].tok(16)
        sems["s_kw"].tok(16)
        sems["s_vw"].tok(16)

        @OP("gpsimd")
        def _(eng, h=h, wq=last["wq"], wk=last["wk"], wv=last["wv"]):
            W(eng, wq)
            eng.dma_start(out=wqb[:].rearrange("p (e n) -> p e n", e=16),
                          in_=qw_d[h]).then_inc(sems["s_qw"].h, 16)
            W(eng, wk)
            eng.dma_start(out=wkb[:].rearrange("p (e n) -> p e n", e=16),
                          in_=kw_d[h]).then_inc(sems["s_kw"].h, 16)
            W(eng, wv)
            eng.dma_start(out=wvb[:].rearrange("p (e n) -> p e n", e=16),
                          in_=vw_d[h]).then_inc(sems["s_vw"].h, 16)

    def make_production(h, xn_tok_fn, interleave=False):
        qsl = h % 2
        units = []
        rd = qkv_ready.setdefault(h, {})
        stt = {"v_ev": None}

        def proj_unit(kind, nb):
            def emit():
                dstmap = {"q": Qb[qsl], "k": Kb[qsl], "v": Vfb}
                wmap = {"q": (wqb, sems["s_qw"]), "k": (wkb, sems["s_kw"]),
                        "v": (wvb, sems["s_vw"])}
                wsb, wsem = wmap[kind]
                wtok = 16 * (h + 1)
                slot = next_psb()
                t_g = pe.tok(1)

                @OP("tensor")
                def _(eng, nb=nb, slot=slot, wsb=wsb, wsem=wsem, wtok=wtok,
                      war=last[f"ps_b{slot}"], xr=xn_tok_fn(nb)):
                    eng.wait_ge(wsem.h, wtok)
                    eng.wait_ge(act.h, xr)
                    W(eng, war)
                    for k in range(16):
                        mm = eng.matmul(ps_b[slot][:],
                                        wsb[:, k * 128:(k + 1) * 128],
                                        bigA_bf[:, k * 2048 + nb * 512: k * 2048 + nb * 512 + 512],
                                        start=(k == 0), stop=(k == 15))
                    mm.then_inc(pe.h, 1)
                if nb == 3:
                    {"q": prod_q_tok, "k": prod_k_tok, "v": prod_v_tok}[kind][h] = t_g

                t_e = act.tok(1)
                hnd = act.h
                scl = SCALE if kind == "q" else None

                @OP("scalar")
                def _(eng, nb=nb, slot=slot, t_g=t_g, dst=dstmap[kind],
                      scl=scl):
                    eng.wait_ge(pe.h, t_g)
                    if scl is None:
                        eng.copy(out=dst[:, nb * 512:(nb + 1) * 512],
                                 in_=ps_b[slot][:]).then_inc(act.h, 1)
                    else:
                        eng.activation(out=dst[:, nb * 512:(nb + 1) * 512],
                                       in_=ps_b[slot][:], func=AF.Copy,
                                       scale=scl).then_inc(act.h, 1)
                last[f"ps_b{slot}"] = (hnd, t_e)
                if nb == 3:
                    rd[kind] = (hnd, t_e)
                if kind == "v":
                    stt["v_ev"] = (hnd, t_e)
            return emit

        def vt_unit(b):
            def emit():
                t_tr = pe.tok(1)

                @OP("tensor")
                def _(eng, b=b, v_ev=stt["v_ev"], war=last["ps_t"], t_misc=t_misc):
                    W(eng, v_ev, war)
                    eng.wait_ge(sems["s_misc"].h, t_misc)
                    for j in range(8):
                        blk = b * 8 + j
                        mm = eng.transpose(ps_t[:, j * 128:(j + 1) * 128],
                                           Vfb[:, blk * 128:(blk + 1) * 128], ident_b)
                    mm.then_inc(pe.h, 1)

                t_ev = act.tok(1)
                last["ps_t"] = (act.h, t_ev)

                @OP("scalar")
                def _(eng, b=b, t_tr=t_tr, qsl=qsl):
                    eng.wait_ge(pe.h, t_tr)
                    dst = Vtb[qsl][:].rearrange("p (c n) -> p c n", c=16)[:, b * 8:(b + 1) * 8, :]
                    src = ps_t[:].rearrange("p (c n) -> p c n", c=8)
                    eng.copy(out=dst, in_=src).then_inc(act.h, 1)
                if b == 1:
                    rd["vt"] = (act.h, t_ev)
            return emit

        if interleave:
            for nb in range(4):
                units.append(proj_unit("q", nb))
                units.append(proj_unit("k", nb))
                units.append(proj_unit("v", nb))
        else:
            for nb in range(4):
                units.append(proj_unit("q", nb))
            for nb in range(4):
                units.append(proj_unit("k", nb))
            for nb in range(4):
                units.append(proj_unit("v", nb))
        units.append(vt_unit(0))
        units.append(vt_unit(1))
        return units

    emit_weight_loads(0)
    units0 = make_production(0, lambda nb: ln1_tok[4 * nb + 3], interleave=True)
    xbuf = [xt[0], xt[1], x1t[0], x1t[1]]
    xkey = ["xt0", "xt1", "x1t0", "x1t1"]
    for i in range(NT):
        sl = i % 4
        t_x = sems[f"s_xa{sl}"].tok(32)
        for hv in range(2):
            u = 2 * i + hv
            qname = ("sync", "scalar")[u % 2] if i < 5 else ("sync", "scalar", "gpsimd")[u % 3]

            @OP(qname)
            def _(eng, i=i, sl=sl, hv=hv, war=last[xkey[sl]]):
                W(eng, war)
                eng.dma_start(out=xbuf[sl][hv * 64:(hv + 1) * 64, :],
                              in_=x_d[i * 128 + hv * 64:i * 128 + (hv + 1) * 64, :]).then_inc(sems[f"s_xa{sl}"].h, 16)

        t_apply = emit_ln_tile(i, xbuf[sl], t_x, sems[f"s_xa{sl}"].h, xn_view, i * 128)
        last[xkey[sl]] = (dve.h, t_apply)
        ln1_tok[i] = act.n
        if i % 4 == 3:
            for u in units0[3 * (i // 4):3 * (i // 4) + 3]:
                u()
    for u in units0[12:]:
        u()
    xn_ready = act.n
    xn_ready_holder[0] = xn_ready

    if debug:
        @OP("sync")
        def _(eng, n=xn_ready):
            eng.wait_ge(act.h, n)
            eng.dma_start(out=dbg["xn"], in_=bigA_bf).then_inc(sems["s_dbg"].h, 16)
        sems["s_dbg"].tok(16)

    attr_tok = {}
    xc_tok = {}
    op_tok = {}
    opadd_tok = {}
    for h in range(H):
        qsl = h % 2
        if h + 1 < H:
            last["wq"] = (pe.h, prod_q_tok[h])
            last["wk"] = (pe.h, prod_k_tok[h])
            last["wv"] = (pe.h, prod_v_tok[h])
            emit_weight_loads(h + 1)
            pending = make_production(h + 1, lambda nb: xn_ready_holder[0])
        else:
            pe_h14_done = pe.n
            t_ow = sems["s_ow"].tok(16 * 16)

            @OP("gpsimd")
            def _(eng, tok=prod_v_tok[H - 1]):
                eng.wait_ge(pe.h, tok)
                for hh in range(16):
                    eng.dma_start(out=bigA_bf[:, hh * 2048:(hh + 1) * 2048],
                                  in_=ow_d[hh * 128:(hh + 1) * 128, :]).then_inc(sems["s_ow"].h, 16)

            def mk_op_unit(tC, nb):
                def emit():
                    slot = next_psb()
                    t_g = pe.tok(1)
                    op_tok[(tC, nb)] = t_g

                    @OP("tensor")
                    def _(eng, tC=tC, nb=nb, slot=slot, t_ar=attr_tok[tC],
                          t_ow=t_ow, war=last[f"ps_b{slot}"]):
                        eng.wait_ge(sems["s_ow"].h, t_ow)
                        eng.wait_ge(sems[f"s_attr{tC}"].h, t_ar)
                        W(eng, war)
                        for hh in range(16):
                            mm = eng.matmul(ps_b[slot][:],
                                            attC[tC][:, hh * 128:(hh + 1) * 128],
                                            bigA_bf[:, hh * 2048 + nb * 512: hh * 2048 + nb * 512 + 512],
                                            start=(hh == 0), stop=(hh == 15))
                        mm.then_inc(pe.h, 1)

                    t_ev = dve.tok(1)
                    last[f"ps_b{slot}"] = (dve.h, t_ev)
                    opadd_tok[(tC, nb)] = t_ev

                    @OP("vector")
                    def _(eng, tC=tC, nb=nb, slot=slot, t_g=t_g,
                          t_xc=xc_tok[tC], war=last[f"x1t{tC}"]):
                        eng.wait_ge(pe.h, t_g)
                        eng.wait_ge(sems[f"s_xc{tC}"].h, t_xc)
                        W(eng, war)
                        eng.tensor_tensor(out=x1t[tC][:, nb * 512:(nb + 1) * 512],
                                          in0=ps_b[slot][:],
                                          in1=xt[tC][:, nb * 512:(nb + 1) * 512],
                                          op=ALU.add).then_inc(dve.h, 1)
                return emit

            pending = [mk_op_unit(tC, nb) for tC in range(2) for nb in range(4)]
        rd = qkv_ready[h]

        probs_b = (probs, probs2)
        diag_b = (diag, diag2)
        pend_prev = None
        for t in range(NT + 1):
            S_units = []
            if t < NT:
                KL = 128 * (t + 1)
                nblk = (KL + 511) // 512
                jd = (t * 128) // 512
                par = t % 2
                war_psa = last["ps_a"]
                mx_toks = []

                def mk_score(j, w, t=t, jd=jd, qsl=qsl, war=war_psa,
                             q_ev=rd["q"], k_ev=rd["k"]):
                    def emit():
                        btok = pe.tok(1)

                        @OP("tensor")
                        def _(eng, j=j, w=w, t=t, jd=jd, qsl=qsl, war=war,
                              q_ev=q_ev, k_ev=k_ev):
                            if j == 0:
                                W(eng, q_ev, k_ev, war)
                                eng.wait_ge(sems["s_misc"].h, t_misc)
                            if j == jd:
                                eng.matmul(ps_a[:, t * 128:(t + 1) * 128],
                                           ident_b, mask_b, start=True, stop=False)
                                mm = eng.matmul(ps_a[:, j * 512:j * 512 + w],
                                                Qb[qsl][:, t * 128:(t + 1) * 128],
                                                Kb[qsl][:, j * 512:j * 512 + w],
                                                start=False, stop=True)
                            else:
                                mm = eng.matmul(ps_a[:, j * 512:j * 512 + w],
                                                Qb[qsl][:, t * 128:(t + 1) * 128],
                                                Kb[qsl][:, j * 512:j * 512 + w],
                                                start=True, stop=True)
                            mm.then_inc(pe.h, 1)

                        mx_toks.append(dve.tok(1))

                        @OP("vector")
                        def _(eng, j=j, w=w, btok=btok):
                            eng.wait_ge(pe.h, btok)
                            eng.reduce_max(out=st_pm[:, j:j + 1],
                                           in_=ps_a[:, j * 512:j * 512 + w],
                                           axis=AX.X).then_inc(dve.h, 1)
                    return emit

                for j in range(nblk):
                    S_units.append(mk_score(j, min(512, KL - j * 512)))

            U_units = []
            if pend_prev is not None:
                pt, ppar, p_dg = pend_prev
                pprobs = probs_b[ppar]
                pdiag = diag_b[ppar]
                nbat = (pt + 4) // 4
                slc = pt % 4
                evt = []
                war_pv = last["ps_c"]
                vt_ev = rd["vt"]
                trk = {"t_tr": None, "t_pv": None}

                def mk_u(b, bw, pt=pt, slc=slc, qsl=qsl, p_dg=p_dg,
                         pprobs=pprobs, pdiag=pdiag, war_pv=war_pv,
                         vt_ev=vt_ev, nbat=nbat):
                    def emit():
                        slot = next_psb()
                        t_tr = pe.tok(1)
                        trk["t_tr"] = t_tr

                        @OP("tensor")
                        def _(eng, b=b, bw=bw, slot=slot, p_dg=p_dg,
                              pprobs=pprobs, pdiag=pdiag, war=last[f"ps_b{slot}"]):
                            eng.wait_ge(act.h, p_dg)
                            W(eng, war)
                            for j in range(bw):
                                kb = b * 4 + j
                                mm = eng.matmul(ps_b[slot][:, j * 128:(j + 1) * 128],
                                                pprobs[:, kb * 128:(kb + 1) * 128],
                                                pdiag, start=True, stop=True)
                            mm.then_inc(pe.h, 1)

                        if b % 2 == 0:
                            t_evt = dve.tok(1)
                            last[f"ps_b{slot}"] = (dve.h, t_evt)
                            evt.append((dve.h, t_evt))

                            @OP("vector")
                            def _(eng, b=b, bw=bw, slot=slot, t_tr=t_tr,
                                  war=last["probsT"]):
                                eng.wait_ge(pe.h, t_tr)
                                W(eng, war)
                                eng.tensor_copy(out=probsT[:, b * 512:b * 512 + bw * 128],
                                                in_=ps_b[slot][:, 0:bw * 128]).then_inc(dve.h, 1)
                        else:
                            t_evt = act.tok(1)
                            last[f"ps_b{slot}"] = (act.h, t_evt)
                            evt.append((act.h, t_evt))

                            @OP("scalar")
                            def _(eng, b=b, bw=bw, slot=slot, t_tr=t_tr,
                                  war=last["probsT"]):
                                eng.wait_ge(pe.h, t_tr)
                                W(eng, war)
                                eng.copy(out=probsT[:, b * 512:b * 512 + bw * 128],
                                         in_=ps_b[slot][:, 0:bw * 128]).then_inc(act.h, 1)

                        if b >= 1:
                            pb = b - 1
                            t_pv = pe.tok(1)
                            trk["t_pv"] = t_pv

                            @OP("tensor")
                            def _(eng, pb=pb, slc=slc, qsl=qsl, ev=evt[pb],
                                  vt_ev=(vt_ev if pb == 0 else None),
                                  war=(war_pv if pb == 0 else None)):
                                W(eng, ev, vt_ev, war)
                                for kb in range(pb * 4, pb * 4 + 4):
                                    mm = eng.matmul(ps_c[:, slc * 128:(slc + 1) * 128],
                                                    Vtb[qsl][:, kb * 128:(kb + 1) * 128],
                                                    probsT[:, kb * 128:(kb + 1) * 128],
                                                    start=(kb == 0), stop=False)
                                mm.then_inc(pe.h, 1)
                    return emit

                for b in range(nbat):
                    U_units.append(mk_u(b, min(4, (pt + 1) - b * 4)))

                def mk_tail(pt=pt, ppar=ppar, slc=slc, qsl=qsl, nbat=nbat,
                            war_pv=war_pv, vt_ev=vt_ev, h=h):
                    def emit():
                        fb = nbat - 1
                        fw = (pt + 1) - fb * 4
                        t_pv = pe.tok(1)

                        @OP("tensor")
                        def _(eng, pt=pt, fb=fb, fw=fw, slc=slc, qsl=qsl,
                              ev=evt[fb], vt_ev=(vt_ev if fb == 0 else None),
                              war=(war_pv if fb == 0 else None)):
                            W(eng, ev, vt_ev, war)
                            for kb in range(fb * 4, fb * 4 + fw):
                                mm = eng.matmul(ps_c[:, slc * 128:(slc + 1) * 128],
                                                Vtb[qsl][:, kb * 128:(kb + 1) * 128],
                                                probsT[:, kb * 128:(kb + 1) * 128],
                                                start=(kb == 0), stop=(kb == pt))
                            mm.then_inc(pe.h, 1)

                        last[f"probs{ppar}"] = (pe.h, trk["t_tr"])
                        last[f"diag{ppar}"] = (pe.h, trk["t_tr"])
                        last["probsT"] = (pe.h, t_pv)

                        if pt % 4 == 3:
                            g4 = pt // 4
                            t_pvev = act.tok(1)
                            last["ps_c"] = (act.h, t_pvev)

                            @OP("scalar")
                            def _(eng, t_pv=t_pv, war=last["atst"]):
                                eng.wait_ge(pe.h, t_pv)
                                W(eng, war)
                                eng.copy(out=atst[:], in_=ps_c[:]).then_inc(act.h, 1)

                            t_w = sems["s_attw"].tok(64)
                            last["atst"] = (sems["s_attw"].h, t_w)

                            @OP("sync")
                            def _(eng, h=h, g4=g4, t_pvev=t_pvev):
                                eng.wait_ge(act.h, t_pvev)
                                for i in range(4):
                                    tt = g4 * 4 + i
                                    eng.dma_start(out=attn_scr[tt][:, h, :],
                                                  in_=atst[:, i * 128:(i + 1) * 128]).then_inc(sems["s_attw"].h, 16)

                            if h == H - 1 and g4 == 0:
                                for tp in range(2):
                                    attr_tok[tp] = sems[f"s_attr{tp}"].tok(16)
                                    xc_tok[tp] = sems[f"s_xc{tp}"].tok(16)

                                    @OP("sync")
                                    def _(eng, tp=tp, aw=sems["s_attw"].n,
                                          ph=pe_h14_done, war_x=last[f"xt{tp}"]):
                                        eng.wait_ge(sems["s_attw"].h, aw)
                                        eng.wait_ge(pe.h, ph)
                                        eng.dma_start(out=attC[tp][:].rearrange("p (c n) -> p c n", c=16),
                                                      in_=attn_scr[tp]).then_inc(sems[f"s_attr{tp}"].h, 16)
                                        W(eng, war_x)
                                        eng.dma_start(out=xt[tp], in_=x_d[tp * 128:(tp + 1) * 128, :]).then_inc(sems[f"s_xc{tp}"].h, 16)
                    return emit

                U_units.append(mk_tail())

            order = U_units[:1]
            si, ui = 0, 1
            while si < len(S_units) or ui < len(U_units):
                if si < len(S_units):
                    order.append(S_units[si])
                    si += 1
                if ui < len(U_units):
                    order.append(U_units[ui])
                    ui += 1
            for u in order:
                u()

            if t < NT:
                t_nb = dve.tok(1)

                @OP("vector")
                def _(eng, t=t, nblk=nblk, t_last=mx_toks[-1]):
                    eng.wait_ge(dve.h, t_last)
                    eng.reduce_max(out=st_negb[:, t:t + 1], in_=st_pm[:, 0:nblk],
                                   axis=AX.X, negate=True).then_inc(dve.h, 1)

                t_exp = act.tok(1)
                last["ps_a"] = (act.h, t_exp)

                @OP("scalar")
                def _(eng, t=t, KL=KL, par=par, t_nb=t_nb,
                      war=last[f"probs{par}"]):
                    eng.wait_ge(dve.h, t_nb)
                    W(eng, war)
                    eng.activation(out=probs_b[par][:, 0:KL], in_=ps_a[:, 0:KL],
                                   func=AF.Exp, bias=st_negb[:, t:t + 1],
                                   accum_out=st_ssum[:, t:t + 1]).then_inc(act.h, 1)

                t_rcp = dve.tok(1)

                @OP("vector")
                def _(eng, t=t, t_exp=t_exp, t_rcp=t_rcp):
                    eng.wait_ge(act.h, t_exp)
                    eng.reciprocal(out=st_rcp[:, t:t + 1],
                                   in_=st_ssum[:, t:t + 1]).then_inc(dve.h, 1)

                t_dg = act.tok(1)

                @OP("scalar")
                def _(eng, t=t, par=par, t_rcp=t_rcp, war=last[f"diag{par}"],
                      t_misc=t_misc):
                    eng.wait_ge(dve.h, t_rcp)
                    W(eng, war)
                    eng.wait_ge(sems["s_misc"].h, t_misc)
                    eng.activation(out=diag_b[par][:], in_=ident_b[:],
                                   func=AF.Copy,
                                   scale=st_rcp[:, t:t + 1]).then_inc(act.h, 1)

                if pending and (h < H - 1 or t >= 5):
                    pending.pop(0)()

                pend_prev = (t, par, t_dg)
        for u in pending:
            u()

    barrier(engines=("tensor", "vector", "scalar"))
    xn2_view = xn2h[:].rearrange("p (e n) -> p e n", e=16)
    ln2_tok = {}
    ln2c_apply = {}
    if debug:
        @OP("sync")
        def _(eng, aw=sems["s_attw"].n):
            eng.wait_ge(sems["s_attw"].h, aw)
            eng.dma_start(out=dbg["at"], in_=attn_scr).then_inc(sems["s_dbg"].h, 16)
        sems["s_dbg"].tok(16)

    for t in range(NT):
        sl = t % 2
        if t < 2:
            t_g_last = op_tok[(t, 3)]
            t_ev_last = opadd_tok[(t, 3)]
        else:
            t_xc = sems[f"s_xc{sl}"].tok(16)
            t_ar = sems[f"s_attr{sl}"].tok(16)

            @OP("sync")
            def _(eng, t=t, sl=sl, war_a=last[f"attC{sl}"]):
                W(eng, war_a)
                eng.dma_start(out=attC[sl][:].rearrange("p (c n) -> p c n", c=16),
                              in_=attn_scr[t]).then_inc(sems[f"s_attr{sl}"].h, 16)

            @OP("scalar")
            def _(eng, t=t, sl=sl, war_x=last[f"xt{sl}"]):
                W(eng, war_x)
                eng.dma_start(out=xt[sl], in_=x_d[t * 128:(t + 1) * 128, :]).then_inc(sems[f"s_xc{sl}"].h, 16)

            t_g_last = None
            t_ev_last = None
            for nb in range(4):
                slot = next_psb()
                t_g = pe.tok(1)
                t_g_last = t_g

                @OP("tensor")
                def _(eng, t=t, nb=nb, sl=sl, slot=slot, t_ar=t_ar, t_ow=t_ow,
                      war=last[f"ps_b{slot}"]):
                    eng.wait_ge(sems["s_ow"].h, t_ow)
                    eng.wait_ge(sems[f"s_attr{sl}"].h, t_ar)
                    W(eng, war)
                    for hh in range(16):
                        mm = eng.matmul(ps_b[slot][:],
                                        attC[sl][:, hh * 128:(hh + 1) * 128],
                                        bigA_bf[:, hh * 2048 + nb * 512: hh * 2048 + nb * 512 + 512],
                                        start=(hh == 0), stop=(hh == 15))
                    mm.then_inc(pe.h, 1)

                t_ev = dve.tok(1)
                last[f"ps_b{slot}"] = (dve.h, t_ev)
                t_ev_last = t_ev

                @OP("vector")
                def _(eng, t=t, nb=nb, sl=sl, slot=slot, t_g=t_g, t_xc=t_xc,
                      war=last[f"x1t{sl}"]):
                    eng.wait_ge(pe.h, t_g)
                    eng.wait_ge(sems[f"s_xc{sl}"].h, t_xc)
                    W(eng, war)
                    eng.tensor_tensor(out=x1t[sl][:, nb * 512:(nb + 1) * 512],
                                      in0=ps_b[slot][:], in1=xt[sl][:, nb * 512:(nb + 1) * 512],
                                      op=ALU.add).then_inc(dve.h, 1)

        last[f"attC{sl}"] = (pe.h, t_g_last)
        last[f"xt{sl}"] = (dve.h, t_ev_last)
        t_x1 = sems[f"s_x1w{sl}"].tok(16)
        last[f"x1t{sl}"] = (sems[f"s_x1w{sl}"].h, t_x1)

        @OP("gpsimd")
        def _(eng, t=t, sl=sl, t_ev=t_ev_last):
            eng.wait_ge(dve.h, t_ev)
            eng.dma_start(out=x1_scr[t * 128:(t + 1) * 128, :], in_=x1t[sl][:]).then_inc(sems[f"s_x1w{sl}"].h, 16)

        if t < 8:
            t_ap2 = emit_ln_tile(t, x1t[sl], t_ev_last, dve.h, xn2_view,
                                 t * 128)
            ln2c_apply[sl] = (dve.h, t_ap2)
    ln2_tok[0] = act.n

    if debug:
        @OP("sync")
        def _(eng, n0=sems["s_x1w0"].n, n1=sems["s_x1w1"].n):
            eng.wait_ge(sems["s_x1w0"].h, n0)
            eng.wait_ge(sems["s_x1w1"].h, n1)
            eng.dma_start(out=dbg["x1"], in_=x1_scr).then_inc(sems["s_dbg"].h, 16)
        sems["s_dbg"].tok(16)

    barrier(extra=(sems["s_x1w0"], sems["s_x1w1"]))

    y_toks = {}
    for hf in range(2):
        toff = hf * 8

        dai_tok = {}
        for i in range(8):
            tt = toff + i
            dai_tok[i] = sems["s_dai"].tok(16)

            @OP("sync")
            def _(eng, tt=tt, i=i, war_y=(y_toks.get(i) if hf else None)):
                if war_y:
                    eng.wait_ge(sems["s_y"].h, war_y)
                eng.dma_start(out=bigA[:, i * 2048:(i + 1) * 2048],
                              in_=x1_scr[tt * 128:(tt + 1) * 128, :]).then_inc(sems["s_dai"].h, 16)

        ln2_act = ln2_tok[hf]

        for g in range(NGRP):
            t_mul_last = None
            for hcg in range(HGRP):
                hc = g * HGRP + hcg
                wsl = hc % 2
                t_gww = sems[f"s_gw{wsl}"].tok(16)
                t_uww = sems[f"s_uw{wsl}"].tok(16)
                sems["s_dw"].tok(16)

                @OP("gpsimd")
                def _(eng, hc=hc, hcg=hcg, wsl=wsl, war_g=last[f"gw{wsl}"],
                      war_u=last[f"uw{wsl}"], war_d=last["dwg"] if hcg == 0 else None):
                    W(eng, war_g)
                    eng.dma_start(out=gwb[wsl][:].rearrange("p (e n) -> p e n", e=16),
                                  in_=gw_d[hc]).then_inc(sems[f"s_gw{wsl}"].h, 16)
                    W(eng, war_u)
                    eng.dma_start(out=uwb[wsl][:].rearrange("p (e n) -> p e n", e=16),
                                  in_=uw_d[hc]).then_inc(sems[f"s_uw{wsl}"].h, 16)
                    if war_d:
                        W(eng, war_d)
                    eng.dma_start(out=dwg[:, hcg * 2048:(hcg + 1) * 2048],
                                  in_=dw_d[hc]).then_inc(sems["s_dw"].h, 16)

                gts, uts = [], []
                for ch in range(2):
                    go = ch * 1024
                    uo = ch * 1024 + 512
                    t_gate = pe.tok(1)
                    gts.append(t_gate)

                    @OP("tensor")
                    def _(eng, ch=ch, wsl=wsl, go=go, t_w=t_gww,
                          ln2_act=ln2_act, war=last[f"ps_f{ch}"]):
                        eng.wait_ge(sems[f"s_gw{wsl}"].h, t_w)
                        eng.wait_ge(act.h, ln2_act)
                        W(eng, war)
                        for k in range(16):
                            mm = eng.matmul(ps_a[:, go:go + 512],
                                            gwb[wsl][:, k * 128:(k + 1) * 128],
                                            xn2h[:, k * 1024 + ch * 512: k * 1024 + ch * 512 + 512],
                                            start=(k == 0), stop=(k == 15))
                        mm.then_inc(pe.h, 1)

                    t_up = pe.tok(1)
                    uts.append(t_up)

                    @OP("tensor")
                    def _(eng, ch=ch, wsl=wsl, uo=uo, t_w=t_uww,
                          war=last[f"ps_f{ch}"]):
                        eng.wait_ge(sems[f"s_uw{wsl}"].h, t_w)
                        W(eng, war)
                        for k in range(16):
                            mm = eng.matmul(ps_a[:, uo:uo + 512],
                                            uwb[wsl][:, k * 128:(k + 1) * 128],
                                            xn2h[:, k * 1024 + ch * 512: k * 1024 + ch * 512 + 512],
                                            start=(k == 0), stop=(k == 15))
                        mm.then_inc(pe.h, 1)

                    t_sil = act.tok(1)

                    @OP("scalar")
                    def _(eng, ch=ch, go=go, t_gate=t_gate, war=last[f"sil{ch}"]):
                        eng.wait_ge(pe.h, t_gate)
                        W(eng, war)
                        eng.activation(out=sil[:, ch * 512:(ch + 1) * 512],
                                       in_=ps_a[:, go:go + 512],
                                       func=AF.Silu).then_inc(act.h, 1)

                    t_mul = dve.tok(1)
                    t_mul_last = t_mul
                    last[f"sil{ch}"] = (dve.h, t_mul)
                    last[f"ps_f{ch}"] = (dve.h, t_mul)

                    @OP("vector")
                    def _(eng, hcg=hcg, ch=ch, uo=uo, t_up=t_up, t_sil=t_sil,
                          war=last["hid"] if (hcg == 0 and ch == 0) else None):
                        eng.wait_ge(pe.h, t_up)
                        eng.wait_ge(act.h, t_sil)
                        if war:
                            W(eng, war)
                        eng.tensor_tensor(out=hid[:, hcg * 1024 + ch * 512: hcg * 1024 + ch * 512 + 512],
                                          in0=sil[:, ch * 512:(ch + 1) * 512],
                                          in1=ps_a[:, uo:uo + 512],
                                          op=ALU.mult).then_inc(dve.h, 1)

                last[f"gw{wsl}"] = (pe.h, gts[1])
                last[f"uw{wsl}"] = (pe.h, uts[1])
                last_up_tok = uts[1]

            t_dw_w = sems["s_dw"].n
            t_dn_last = None
            t_add_by = {}
            for ti in range(8):
                for nb in range(4):
                    slot = next_psb()
                    t_dn = pe.tok(1)
                    t_dn_last = t_dn

                    @OP("tensor")
                    def _(eng, ti=ti, nb=nb, slot=slot, t_dw_w=t_dw_w,
                          t_mul=t_mul_last, war=last[f"ps_b{slot}"]):
                        eng.wait_ge(sems["s_dw"].h, t_dw_w)
                        eng.wait_ge(dve.h, t_mul)
                        W(eng, war)
                        for hcg in range(HGRP):
                            mm = eng.matmul(ps_b[slot][:],
                                            hid[:, hcg * 1024 + ti * 128: hcg * 1024 + ti * 128 + 128],
                                            dwg[:, hcg * 2048 + nb * 512: hcg * 2048 + nb * 512 + 512],
                                            start=(hcg == 0), stop=(hcg == HGRP - 1))
                        mm.then_inc(pe.h, 1)

                    t_add = dve.tok(1)
                    last[f"ps_b{slot}"] = (dve.h, t_add)
                    t_add_by[ti] = t_add

                    @OP("vector")
                    def _(eng, ti=ti, nb=nb, slot=slot, t_dn=t_dn, t_dai=dai_tok[ti]):
                        eng.wait_ge(pe.h, t_dn)
                        eng.wait_ge(sems["s_dai"].h, t_dai)
                        dst = bigA[:, ti * 2048 + nb * 512: ti * 2048 + (nb + 1) * 512]
                        eng.tensor_tensor(out=dst, in0=dst, in1=ps_b[slot][:],
                                          op=ALU.add).then_inc(dve.h, 1)

                if hf == 0 and g == NGRP - 1:
                    tt2 = 8 + ti
                    sl2 = ti % 2
                    t_x1r = sems[f"s_x1r{sl2}"].tok(16)

                    @OP("sync")
                    def _(eng, tt2=tt2, sl2=sl2, war=last[f"x1t{sl2}"],
                          war2=ln2c_apply.get(sl2)):
                        W(eng, war, war2)
                        eng.dma_start(out=x1t[sl2], in_=x1_scr[tt2 * 128:(tt2 + 1) * 128, :]).then_inc(sems[f"s_x1r{sl2}"].h, 16)

                    t_ap = emit_ln_tile(ti, x1t[sl2], t_x1r,
                                        sems[f"s_x1r{sl2}"].h, xn2_view,
                                        ti * 128, dst_war=(pe.h, last_up_tok))
                    last[f"x1t{sl2}"] = (dve.h, t_ap)

            last["hid"] = (pe.h, t_dn_last)
            last["dwg"] = (pe.h, t_dn_last)
            if g == NGRP - 1:
                final_adds = dict(t_add_by)
        if hf == 0:
            ln2_tok[1] = act.n

        for ti in range(8):
            tt = toff + ti
            y_toks[ti] = sems["s_y"].tok(16)

            @OP("sync")
            def _(eng, tt=tt, ti=ti, t_add=final_adds[ti]):
                eng.wait_ge(dve.h, t_add)
                eng.dma_start(out=y_d[tt * 128:(tt + 1) * 128, :],
                              in_=bigA[:, ti * 2048:(ti + 1) * 2048]).then_inc(sems["s_y"].h, 16)

    @OP("sync")
    def _(eng, n=sems["s_y"].n, nd=sems["s_dbg"].n):
        eng.wait_ge(sems["s_y"].h, n)
        if nd:
            eng.wait_ge(sems["s_dbg"].h, nd)

    with stack:
        with nc.Block() as block:
            def runner(ename):
                def f(eng):
                    for fn in steps[ename]:
                        fn(eng)
                return f
            block.sync(runner("sync"))
            block.tensor(runner("tensor"))
            block.vector(runner("vector"))
            block.scalar(runner("scalar"))
            block.gpsimd(runner("gpsimd"))
    return nc


def _prep_inputs(x, q_wei, k_wei, v_wei, out_wei, w_gate, w_up, w_down):
    asc = np.ascontiguousarray
    qw = asc(np.asarray(q_wei, np.float32).reshape(16, 128, 16, 128).transpose(2, 1, 0, 3))
    kw = asc(np.asarray(k_wei, np.float32).reshape(16, 128, 16, 128).transpose(2, 1, 0, 3))
    vw = asc(np.asarray(v_wei, np.float32).reshape(16, 128, 16, 128).transpose(2, 1, 0, 3))
    ow = asc(np.asarray(out_wei, np.float32))
    gw = asc(np.asarray(w_gate, np.float32).reshape(64, 128, 16, 128).transpose(0, 3, 2, 1))
    uw = asc(np.asarray(w_up, np.float32).reshape(64, 128, 16, 128).transpose(0, 3, 2, 1))
    dw = asc(np.asarray(w_down, np.float32).T.reshape(64, 128, 2048))
    mask = np.triu(np.full((128, 128), -66504.0, dtype=np.float32), k=1)
    ident = np.eye(128, dtype=np.float32)
    x = np.asarray(x, np.float32)
    maps = []
    for c in range(8):
        maps.append(dict(x=asc(x[c]), qw=qw, kw=kw, vw=vw, ow=ow,
                         gw=gw, uw=uw, dw=dw, mask=mask, ident=ident))
    return maps


def kernel(L0_attn_hidden_states, L0_attn_q_wei, L0_attn_k_wei, L0_attn_v_wei,
           L0_attn_out_wei, L0_ffn_w_gate, L0_ffn_w_up, L0_ffn_w_down):
    maps = _prep_inputs(L0_attn_hidden_states, L0_attn_q_wei, L0_attn_k_wei,
                        L0_attn_v_wei, L0_attn_out_wei, L0_ffn_w_gate,
                        L0_ffn_w_up, L0_ffn_w_down)
    nc = build(debug=False)
    res = run_bass_kernel_spmd(nc, maps, core_ids=list(range(8)))
    return np.stack([np.asarray(r["y"]) for r in res.results], axis=0)


# version 14
# speedup vs baseline: 1.0914x; 1.0067x over previous
import math
import numpy as np
import concourse.bass as bass
import concourse.mybir as mybir
from concourse.bass_utils import run_bass_kernel_spmd

F32 = mybir.dt.float32
BF16 = mybir.dt.bfloat16
AF = mybir.ActivationFunctionType
ALU = mybir.AluOpType
AX = mybir.AxisListType

S = 2048
E = 2048
H = 16
D = 128
NT = 16
NHC = 64
SCALE = float(1.0 / math.sqrt(float(D)))
EPS = 1e-5
HGRP = 8
NGRP = NHC // HGRP

ENGINES = ("sync", "tensor", "vector", "scalar", "gpsimd")


class SemT:
    def __init__(self, h):
        self.h = h
        self.n = 0

    def tok(self, k):
        self.n += k
        return self.n


def build(debug=False):
    nc = bass.Bass("TRN2", target_bir_lowering=False)

    x_d = nc.dram_tensor("x", [S, E], F32, kind="ExternalInput").ap()
    qw_d = nc.dram_tensor("qw", [H, 128, 16, 128], F32, kind="ExternalInput").ap()
    kw_d = nc.dram_tensor("kw", [H, 128, 16, 128], F32, kind="ExternalInput").ap()
    vw_d = nc.dram_tensor("vw", [H, 128, 16, 128], F32, kind="ExternalInput").ap()
    ow_d = nc.dram_tensor("ow", [E, E], F32, kind="ExternalInput").ap()
    gw_d = nc.dram_tensor("gw", [NHC, 128, 16, 128], F32, kind="ExternalInput").ap()
    uw_d = nc.dram_tensor("uw", [NHC, 128, 16, 128], F32, kind="ExternalInput").ap()
    dw_d = nc.dram_tensor("dw", [NHC, 128, E], F32, kind="ExternalInput").ap()
    mask_d = nc.dram_tensor("mask", [128, 128], F32, kind="ExternalInput").ap()
    ident_d = nc.dram_tensor("ident", [128, 128], F32, kind="ExternalInput").ap()
    y_d = nc.dram_tensor("y", [S, E], F32, kind="ExternalOutput").ap()
    attn_scr = nc.dram_tensor("attn_scr", [NT, 128, H, 128], BF16).ap()
    x1_scr = nc.dram_tensor("x1_scr", [S, E], F32).ap()
    dbg = {}
    if debug:
        dbg["xn"] = nc.dram_tensor("d_xn", [128, 32768], BF16, kind="ExternalOutput").ap()
        dbg["at"] = nc.dram_tensor("d_at", [NT, 128, H, 128], BF16, kind="ExternalOutput").ap()
        dbg["x1"] = nc.dram_tensor("d_x1", [S, E], F32, kind="ExternalOutput").ap()
        dbg["xn2"] = nc.dram_tensor("d_xn2", [128, 16384], BF16, kind="ExternalOutput").ap()

    MAP_BYTES = 208896
    guard = nc.alloc_sbuf_tensor("manual_region", [128, MAP_BYTES], mybir.dt.uint8)
    SB_BASE = nc.lookup_mloc(guard).addr

    def sb(name, shape, dtype, off):
        nb = int(np.prod(shape[1:])) * (2 if dtype == BF16 else 4)
        assert off + nb <= MAP_BYTES, (name, off, nb)
        return nc.alloc_sbuf_tensor_at(name, shape, dtype, offset=SB_BASE + off).ap()

    bigA = sb("bigA", [128, 16384], F32, 0)
    bigA_bf = bigA.bitcast(BF16)
    Qb = [sb(f"Qb{i}", [128, 2048], BF16, 65536 + i * 4096) for i in range(2)]
    Kb = [sb(f"Kb{i}", [128, 2048], BF16, 73728 + i * 4096) for i in range(2)]
    Vtb = [sb(f"Vtb{i}", [128, 2048], BF16, 81920 + i * 4096) for i in range(2)]
    Vfb = sb("Vfb", [128, 2048], BF16, 90112)
    probs = sb("probs", [128, 2048], BF16, 94208)
    probsT = sb("probsT", [128, 2048], BF16, 98304)
    diag = sb("diag", [128, 128], BF16, 102400)
    atst = sb("atst", [128, 512], BF16, 102656)
    attC = [sb(f"attC{i}", [128, 2048], BF16, 114688 + i * 4096) for i in range(2)]
    xn2h = sb("xn2h", [128, 16384], BF16, 65536)
    ident_b = sb("ident_b", [128, 128], BF16, 103680)
    mask_b = sb("mask_b", [128, 128], BF16, 103936)
    st_negmean = sb("st_negmean", [128, 16], F32, 104192)
    st_ssq = sb("st_ssq", [128, 16], F32, 104256)
    st_lnv = sb("st_lnv", [128, 16], F32, 104320)
    st_rstd = sb("st_rstd", [128, 16], F32, 104384)
    st_negmax = sb("st_negmax", [128, 16], F32, 104448)
    st_negb = sb("st_negb", [128, 16], F32, 104512)
    st_ssum = sb("st_ssum", [128, 16], F32, 104576)
    st_lns = sb("st_lns", [128, 16], F32, 104640)
    st_rcp = sb("st_rcp", [128, 16], F32, 104704)
    epsb = sb("epsb", [128, 1], F32, 104768)
    st_pm = sb("st_pm", [128, 4], F32, 104832)
    probs2 = sb("probs2", [128, 2048], BF16, 104960)
    diag2 = sb("diag2", [128, 128], BF16, 109056)
    wqb = sb("wqb", [128, 2048], BF16, 114688)
    wkb = sb("wkb", [128, 2048], BF16, 118784)
    wvb = sb("wvb", [128, 2048], BF16, 122880)
    dwg = sb("dwg", [128, 16384], BF16, 114688)
    xt = [sb(f"xt{i}", [128, 2048], F32, 147456 + i * 8192) for i in range(2)]
    gwb = [sb(f"gwb{i}", [128, 2048], BF16, 147456 + i * 4096) for i in range(2)]
    uwb = [sb(f"uwb{i}", [128, 2048], BF16, 155648 + i * 4096) for i in range(2)]
    x1t = [sb(f"x1t{i}", [128, 2048], F32, 163840 + i * 8192) for i in range(2)]
    hid = sb("hid", [128, 8192], BF16, 180224)
    sq = sb("sq", [128, 2048], F32, 180224)
    xntm = sb("xntm", [128, 2048], BF16, 196608)
    sil = sb("sil", [128, 1024], F32, 200704)

    ps_a = nc.alloc_psum_tensor("ps_a", [128, 2048], F32).ap()
    ps_b = [nc.alloc_psum_tensor(f"ps_b{i}", [128, 512], F32).ap() for i in range(2)]
    ps_t = nc.alloc_psum_tensor("ps_t", [128, 1024], BF16).ap()
    ps_c = nc.alloc_psum_tensor("ps_c", [128, 512], F32).ap()

    import contextlib
    stack = contextlib.ExitStack()
    sems = {}
    for name in ("pe", "dve", "act", "s_misc", "s_xa0", "s_xa1", "s_xa2",
                 "s_xa3", "s_qw", "s_kw",
                 "s_vw", "s_attw", "s_ow", "s_attr0", "s_attr1", "s_xc0", "s_xc1",
                 "s_x1w0", "s_x1w1", "s_x1r0", "s_x1r1", "s_dai",
                 "s_gw0", "s_gw1", "s_uw0", "s_uw1", "s_dw", "s_y", "s_dbg"):
        sems[name] = SemT(stack.enter_context(nc.semaphore(name)))
    pe, dve, act = sems["pe"], sems["dve"], sems["act"]

    steps = {e: [] for e in ENGINES}

    def OP(e):
        def deco(fn):
            steps[e].append(fn)
        return deco

    def barrier(extra=(), engines=ENGINES):
        targets = [(s.h, s.n) for s in (pe, dve, act)] + [(s.h, s.n) for s in extra]
        targets = [(h, n) for h, n in targets if n > 0]
        for e in engines:
            def f(eng, targets=tuple(targets)):
                for h, n in targets:
                    eng.wait_ge(h, n)
            steps[e].append(f)

    last = {
        "ps_t": None, "ps_a": None, "ps_b0": None, "ps_b1": None, "ps_c": None,
        "ps_f0": None, "ps_f1": None, "sil0": None, "sil1": None,
        "probs0": None, "probs1": None, "probsT": None,
        "diag0": None, "diag1": None, "atst": None,
        "xntm": None, "sil": None, "hid": None, "dwg": None,
        "xt0": None, "xt1": None, "x1t0": None, "x1t1": None,
        "wq": None, "wk": None, "wv": None,
        "gw0": None, "gw1": None, "uw0": None, "uw1": None,
        "attC0": None, "attC1": None,
    }

    def W(eng, *keys):
        for k in keys:
            v = last[k] if isinstance(k, str) else k
            if v:
                eng.wait_ge(v[0], v[1])

    xn_view = bigA_bf.rearrange("p (e n) -> p e n", e=16)

    @OP("gpsimd")
    def _(eng):
        eng.dma_start(out=ident_b, in_=ident_d).then_inc(sems["s_misc"].h, 16)
        eng.dma_start(out=mask_b, in_=mask_d).then_inc(sems["s_misc"].h, 16)
    t_misc = sems["s_misc"].tok(32)

    t_eps = dve.tok(1)

    @OP("vector")
    def _(eng):
        eng.memset(epsb[:], EPS).then_inc(dve.h, 1)

    ln1_tok = {}

    def emit_ln_tile(i, src_tile, t_src, src_sem, dst_view, dst_col0,
                     dst_war=None):
        t_red = dve.tok(1)
        t_nm = dve.tok(1)

        @OP("vector")
        def _(eng, i=i, src_tile=src_tile, t_src=t_src, src_sem=src_sem,
              t_red=t_red):
            eng.wait_ge(src_sem, t_src)
            eng.reduce_sum(out=st_negmean[:, i:i + 1], in_=src_tile[:], axis=AX.X).then_inc(dve.h, 1)
            eng.wait_ge(dve.h, t_red)
            eng.tensor_scalar_mul(out=st_negmean[:, i:i + 1], in0=st_negmean[:, i:i + 1],
                                  scalar1=-1.0 / E).then_inc(dve.h, 1)

        t_ssq = act.tok(1)
        t_lnv = act.tok(1)
        t_rstd = act.tok(1)

        @OP("scalar")
        def _(eng, i=i, src_tile=src_tile, t_nm=t_nm, t_ssq=t_ssq,
              t_lnv=t_lnv, t_eps=t_eps, war=last["xntm"]):
            eng.wait_ge(dve.h, t_nm)
            W(eng, war)
            eng.activation(out=xntm[:], in_=src_tile[:], func=AF.Square,
                           bias=st_negmean[:, i:i + 1],
                           accum_out=st_ssq[:, i:i + 1]).then_inc(act.h, 1)
            eng.wait_ge(act.h, t_ssq)
            eng.wait_ge(dve.h, t_eps)
            eng.activation(out=st_lnv[:, i:i + 1], in_=st_ssq[:, i:i + 1], func=AF.Ln,
                           scale=1.0 / E, bias=epsb[:, 0:1]).then_inc(act.h, 1)
            eng.wait_ge(act.h, t_lnv)
            eng.activation(out=st_rstd[:, i:i + 1], in_=st_lnv[:, i:i + 1], func=AF.Exp,
                           scale=-0.5).then_inc(act.h, 1)

        t_apply = dve.tok(1)

        @OP("vector")
        def _(eng, i=i, src_tile=src_tile, t_rstd=t_rstd, war=last["xntm"]):
            eng.wait_ge(act.h, t_rstd)
            W(eng, war)
            eng.tensor_scalar(out=xntm[:], in0=src_tile[:], scalar1=st_negmean[:, i:i + 1],
                              scalar2=st_rstd[:, i:i + 1], op0=ALU.add,
                              op1=ALU.mult).then_inc(dve.h, 1)

        for b in range(2):
            t_tr = pe.tok(1)
            last["xntm"] = (pe.h, t_tr)

            @OP("tensor")
            def _(eng, b=b, t_apply=t_apply, t_misc=t_misc, war=last["ps_t"]):
                eng.wait_ge(dve.h, t_apply)
                eng.wait_ge(sems["s_misc"].h, t_misc)
                W(eng, war)
                for j in range(8):
                    e_ch = b * 8 + j
                    mm = eng.transpose(ps_t[:, j * 128:(j + 1) * 128],
                                       xntm[:, e_ch * 128:(e_ch + 1) * 128], ident_b)
                mm.then_inc(pe.h, 1)

            if b == 0:
                t_ev = act.tok(1)
                last["ps_t"] = (act.h, t_ev)

                @OP("scalar")
                def _(eng, b=b, t_tr=t_tr, dst_view=dst_view, dst_col0=dst_col0,
                      dst_war=dst_war):
                    eng.wait_ge(pe.h, t_tr)
                    W(eng, dst_war)
                    dst = dst_view[:, b * 8:(b + 1) * 8, dst_col0:dst_col0 + 128]
                    src = ps_t[:].rearrange("p (e n) -> p e n", e=8)
                    eng.copy(out=dst, in_=src).then_inc(act.h, 1)
            else:
                t_ev = dve.tok(1)
                last["ps_t"] = (dve.h, t_ev)

                @OP("vector")
                def _(eng, b=b, t_tr=t_tr, dst_view=dst_view, dst_col0=dst_col0,
                      dst_war=dst_war):
                    eng.wait_ge(pe.h, t_tr)
                    W(eng, dst_war)
                    dst = dst_view[:, b * 8:(b + 1) * 8, dst_col0:dst_col0 + 128]
                    src = ps_t[:].rearrange("p (e n) -> p e n", e=8)
                    eng.tensor_copy(out=dst, in_=src).then_inc(dve.h, 1)
        return t_apply

    psb_rot = [0]

    def next_psb():
        s_ = psb_rot[0]
        psb_rot[0] ^= 1
        return s_

    prod_q_tok = {}
    prod_k_tok = {}
    prod_v_tok = {}
    qkv_ready = {}
    xn_ready_holder = [0]

    def emit_weight_loads(h):
        sems["s_qw"].tok(16)
        sems["s_kw"].tok(16)
        sems["s_vw"].tok(16)

        @OP("gpsimd")
        def _(eng, h=h, wq=last["wq"], wk=last["wk"], wv=last["wv"]):
            W(eng, wq)
            eng.dma_start(out=wqb[:].rearrange("p (e n) -> p e n", e=16),
                          in_=qw_d[h]).then_inc(sems["s_qw"].h, 16)
            W(eng, wk)
            eng.dma_start(out=wkb[:].rearrange("p (e n) -> p e n", e=16),
                          in_=kw_d[h]).then_inc(sems["s_kw"].h, 16)
            W(eng, wv)
            eng.dma_start(out=wvb[:].rearrange("p (e n) -> p e n", e=16),
                          in_=vw_d[h]).then_inc(sems["s_vw"].h, 16)

    def make_production(h, xn_tok_fn, interleave=False):
        qsl = h % 2
        units = []
        rd = qkv_ready.setdefault(h, {})
        stt = {"v_ev": None}

        def proj_unit(kind, nb):
            def emit():
                dstmap = {"q": Qb[qsl], "k": Kb[qsl], "v": Vfb}
                wmap = {"q": (wqb, sems["s_qw"]), "k": (wkb, sems["s_kw"]),
                        "v": (wvb, sems["s_vw"])}
                wsb, wsem = wmap[kind]
                wtok = 16 * (h + 1)
                slot = next_psb()
                t_g = pe.tok(1)

                @OP("tensor")
                def _(eng, nb=nb, slot=slot, wsb=wsb, wsem=wsem, wtok=wtok,
                      war=last[f"ps_b{slot}"], xr=xn_tok_fn(nb)):
                    eng.wait_ge(wsem.h, wtok)
                    eng.wait_ge(act.h, xr)
                    W(eng, war)
                    for k in range(16):
                        mm = eng.matmul(ps_b[slot][:],
                                        wsb[:, k * 128:(k + 1) * 128],
                                        bigA_bf[:, k * 2048 + nb * 512: k * 2048 + nb * 512 + 512],
                                        start=(k == 0), stop=(k == 15))
                    mm.then_inc(pe.h, 1)
                if nb == 3:
                    {"q": prod_q_tok, "k": prod_k_tok, "v": prod_v_tok}[kind][h] = t_g

                t_e = act.tok(1)
                hnd = act.h
                scl = SCALE if kind == "q" else None

                @OP("scalar")
                def _(eng, nb=nb, slot=slot, t_g=t_g, dst=dstmap[kind],
                      scl=scl):
                    eng.wait_ge(pe.h, t_g)
                    if scl is None:
                        eng.copy(out=dst[:, nb * 512:(nb + 1) * 512],
                                 in_=ps_b[slot][:]).then_inc(act.h, 1)
                    else:
                        eng.activation(out=dst[:, nb * 512:(nb + 1) * 512],
                                       in_=ps_b[slot][:], func=AF.Copy,
                                       scale=scl).then_inc(act.h, 1)
                last[f"ps_b{slot}"] = (hnd, t_e)
                if nb == 3:
                    rd[kind] = (hnd, t_e)
                if kind == "v":
                    stt["v_ev"] = (hnd, t_e)
            return emit

        def vt_unit(b):
            def emit():
                t_tr = pe.tok(1)

                @OP("tensor")
                def _(eng, b=b, v_ev=stt["v_ev"], war=last["ps_t"], t_misc=t_misc):
                    W(eng, v_ev, war)
                    eng.wait_ge(sems["s_misc"].h, t_misc)
                    for j in range(8):
                        blk = b * 8 + j
                        mm = eng.transpose(ps_t[:, j * 128:(j + 1) * 128],
                                           Vfb[:, blk * 128:(blk + 1) * 128], ident_b)
                    mm.then_inc(pe.h, 1)

                t_ev = act.tok(1)
                last["ps_t"] = (act.h, t_ev)

                @OP("scalar")
                def _(eng, b=b, t_tr=t_tr, qsl=qsl):
                    eng.wait_ge(pe.h, t_tr)
                    dst = Vtb[qsl][:].rearrange("p (c n) -> p c n", c=16)[:, b * 8:(b + 1) * 8, :]
                    src = ps_t[:].rearrange("p (c n) -> p c n", c=8)
                    eng.copy(out=dst, in_=src).then_inc(act.h, 1)
                if b == 1:
                    rd["vt"] = (act.h, t_ev)
            return emit

        if interleave:
            for nb in range(4):
                units.append(proj_unit("q", nb))
                units.append(proj_unit("k", nb))
                units.append(proj_unit("v", nb))
        else:
            for nb in range(4):
                units.append(proj_unit("q", nb))
            for nb in range(4):
                units.append(proj_unit("k", nb))
            for nb in range(4):
                units.append(proj_unit("v", nb))
        units.append(vt_unit(0))
        units.append(vt_unit(1))
        return units

    emit_weight_loads(0)
    units0 = make_production(0, lambda nb: ln1_tok[4 * nb + 3], interleave=True)
    xbuf = [xt[0], xt[1], x1t[0], x1t[1]]
    xkey = ["xt0", "xt1", "x1t0", "x1t1"]
    for i in range(NT):
        sl = i % 4
        t_x = sems[f"s_xa{sl}"].tok(16)
        qname = ("sync", "scalar")[i % 2] if i < 4 else ("sync", "scalar", "gpsimd")[(i - 4) % 3]

        @OP(qname)
        def _(eng, i=i, sl=sl, war=last[xkey[sl]]):
            W(eng, war)
            eng.dma_start(out=xbuf[sl], in_=x_d[i * 128:(i + 1) * 128, :]).then_inc(sems[f"s_xa{sl}"].h, 16)

        t_apply = emit_ln_tile(i, xbuf[sl], t_x, sems[f"s_xa{sl}"].h, xn_view, i * 128)
        last[xkey[sl]] = (dve.h, t_apply)
        ln1_tok[i] = act.n
        if i % 4 == 3:
            for u in units0[3 * (i // 4):3 * (i // 4) + 3]:
                u()
    for u in units0[12:]:
        u()
    xn_ready = act.n
    xn_ready_holder[0] = xn_ready

    if debug:
        @OP("sync")
        def _(eng, n=xn_ready):
            eng.wait_ge(act.h, n)
            eng.dma_start(out=dbg["xn"], in_=bigA_bf).then_inc(sems["s_dbg"].h, 16)
        sems["s_dbg"].tok(16)

    attr_tok = {}
    xc_tok = {}
    op_tok = {}
    opadd_tok = {}
    for h in range(H):
        qsl = h % 2
        if h + 1 < H:
            last["wq"] = (pe.h, prod_q_tok[h])
            last["wk"] = (pe.h, prod_k_tok[h])
            last["wv"] = (pe.h, prod_v_tok[h])
            emit_weight_loads(h + 1)
            pending = make_production(h + 1, lambda nb: xn_ready_holder[0])
        else:
            pe_h14_done = pe.n
            t_ow = sems["s_ow"].tok(16 * 16)

            @OP("gpsimd")
            def _(eng, tok=prod_v_tok[H - 1]):
                eng.wait_ge(pe.h, tok)
                for hh in range(16):
                    eng.dma_start(out=bigA_bf[:, hh * 2048:(hh + 1) * 2048],
                                  in_=ow_d[hh * 128:(hh + 1) * 128, :]).then_inc(sems["s_ow"].h, 16)

            def mk_op_unit(tC, nb):
                def emit():
                    slot = next_psb()
                    t_g = pe.tok(1)
                    op_tok[(tC, nb)] = t_g

                    @OP("tensor")
                    def _(eng, tC=tC, nb=nb, slot=slot, t_ar=attr_tok[tC],
                          t_ow=t_ow, war=last[f"ps_b{slot}"]):
                        eng.wait_ge(sems["s_ow"].h, t_ow)
                        eng.wait_ge(sems[f"s_attr{tC}"].h, t_ar)
                        W(eng, war)
                        for hh in range(16):
                            mm = eng.matmul(ps_b[slot][:],
                                            attC[tC][:, hh * 128:(hh + 1) * 128],
                                            bigA_bf[:, hh * 2048 + nb * 512: hh * 2048 + nb * 512 + 512],
                                            start=(hh == 0), stop=(hh == 15))
                        mm.then_inc(pe.h, 1)

                    t_ev = dve.tok(1)
                    last[f"ps_b{slot}"] = (dve.h, t_ev)
                    opadd_tok[(tC, nb)] = t_ev

                    @OP("vector")
                    def _(eng, tC=tC, nb=nb, slot=slot, t_g=t_g,
                          t_xc=xc_tok[tC], war=last[f"x1t{tC}"]):
                        eng.wait_ge(pe.h, t_g)
                        eng.wait_ge(sems[f"s_xc{tC}"].h, t_xc)
                        W(eng, war)
                        eng.tensor_tensor(out=x1t[tC][:, nb * 512:(nb + 1) * 512],
                                          in0=ps_b[slot][:],
                                          in1=xt[tC][:, nb * 512:(nb + 1) * 512],
                                          op=ALU.add).then_inc(dve.h, 1)
                return emit

            pending = [mk_op_unit(tC, nb) for tC in range(2) for nb in range(4)]
        rd = qkv_ready[h]

        probs_b = (probs, probs2)
        diag_b = (diag, diag2)
        pend_prev = None
        for t in range(NT + 1):
            S_units = []
            if t < NT:
                KL = 128 * (t + 1)
                nblk = (KL + 511) // 512
                jd = (t * 128) // 512
                par = t % 2
                war_psa = last["ps_a"]
                mx_toks = []

                def mk_score(j, w, t=t, jd=jd, qsl=qsl, war=war_psa,
                             q_ev=rd["q"], k_ev=rd["k"]):
                    def emit():
                        btok = pe.tok(1)

                        @OP("tensor")
                        def _(eng, j=j, w=w, t=t, jd=jd, qsl=qsl, war=war,
                              q_ev=q_ev, k_ev=k_ev):
                            if j == 0:
                                W(eng, q_ev, k_ev, war)
                                eng.wait_ge(sems["s_misc"].h, t_misc)
                            if j == jd:
                                eng.matmul(ps_a[:, t * 128:(t + 1) * 128],
                                           ident_b, mask_b, start=True, stop=False)
                                mm = eng.matmul(ps_a[:, j * 512:j * 512 + w],
                                                Qb[qsl][:, t * 128:(t + 1) * 128],
                                                Kb[qsl][:, j * 512:j * 512 + w],
                                                start=False, stop=True)
                            else:
                                mm = eng.matmul(ps_a[:, j * 512:j * 512 + w],
                                                Qb[qsl][:, t * 128:(t + 1) * 128],
                                                Kb[qsl][:, j * 512:j * 512 + w],
                                                start=True, stop=True)
                            mm.then_inc(pe.h, 1)

                        mx_toks.append(dve.tok(1))

                        @OP("vector")
                        def _(eng, j=j, w=w, btok=btok):
                            eng.wait_ge(pe.h, btok)
                            eng.reduce_max(out=st_pm[:, j:j + 1],
                                           in_=ps_a[:, j * 512:j * 512 + w],
                                           axis=AX.X).then_inc(dve.h, 1)
                    return emit

                for j in range(nblk):
                    S_units.append(mk_score(j, min(512, KL - j * 512)))

            U_units = []
            if pend_prev is not None:
                pt, ppar, p_dg = pend_prev
                pprobs = probs_b[ppar]
                pdiag = diag_b[ppar]
                nbat = (pt + 4) // 4
                slc = pt % 4
                evt = []
                war_pv = last["ps_c"]
                vt_ev = rd["vt"]
                trk = {"t_tr": None, "t_pv": None}

                def mk_u(b, bw, pt=pt, slc=slc, qsl=qsl, p_dg=p_dg,
                         pprobs=pprobs, pdiag=pdiag, war_pv=war_pv,
                         vt_ev=vt_ev, nbat=nbat):
                    def emit():
                        slot = next_psb()
                        t_tr = pe.tok(1)
                        trk["t_tr"] = t_tr

                        @OP("tensor")
                        def _(eng, b=b, bw=bw, slot=slot, p_dg=p_dg,
                              pprobs=pprobs, pdiag=pdiag, war=last[f"ps_b{slot}"]):
                            eng.wait_ge(act.h, p_dg)
                            W(eng, war)
                            for j in range(bw):
                                kb = b * 4 + j
                                mm = eng.matmul(ps_b[slot][:, j * 128:(j + 1) * 128],
                                                pprobs[:, kb * 128:(kb + 1) * 128],
                                                pdiag, start=True, stop=True)
                            mm.then_inc(pe.h, 1)

                        if b % 2 == 0:
                            t_evt = dve.tok(1)
                            last[f"ps_b{slot}"] = (dve.h, t_evt)
                            evt.append((dve.h, t_evt))

                            @OP("vector")
                            def _(eng, b=b, bw=bw, slot=slot, t_tr=t_tr,
                                  war=last["probsT"]):
                                eng.wait_ge(pe.h, t_tr)
                                W(eng, war)
                                eng.tensor_copy(out=probsT[:, b * 512:b * 512 + bw * 128],
                                                in_=ps_b[slot][:, 0:bw * 128]).then_inc(dve.h, 1)
                        else:
                            t_evt = act.tok(1)
                            last[f"ps_b{slot}"] = (act.h, t_evt)
                            evt.append((act.h, t_evt))

                            @OP("scalar")
                            def _(eng, b=b, bw=bw, slot=slot, t_tr=t_tr,
                                  war=last["probsT"]):
                                eng.wait_ge(pe.h, t_tr)
                                W(eng, war)
                                eng.copy(out=probsT[:, b * 512:b * 512 + bw * 128],
                                         in_=ps_b[slot][:, 0:bw * 128]).then_inc(act.h, 1)

                        if b >= 1:
                            pb = b - 1
                            t_pv = pe.tok(1)
                            trk["t_pv"] = t_pv

                            @OP("tensor")
                            def _(eng, pb=pb, slc=slc, qsl=qsl, ev=evt[pb],
                                  vt_ev=(vt_ev if pb == 0 else None),
                                  war=(war_pv if pb == 0 else None)):
                                W(eng, ev, vt_ev, war)
                                for kb in range(pb * 4, pb * 4 + 4):
                                    mm = eng.matmul(ps_c[:, slc * 128:(slc + 1) * 128],
                                                    Vtb[qsl][:, kb * 128:(kb + 1) * 128],
                                                    probsT[:, kb * 128:(kb + 1) * 128],
                                                    start=(kb == 0), stop=False)
                                mm.then_inc(pe.h, 1)
                    return emit

                for b in range(nbat):
                    U_units.append(mk_u(b, min(4, (pt + 1) - b * 4)))

                def mk_tail(pt=pt, ppar=ppar, slc=slc, qsl=qsl, nbat=nbat,
                            war_pv=war_pv, vt_ev=vt_ev, h=h):
                    def emit():
                        fb = nbat - 1
                        fw = (pt + 1) - fb * 4
                        t_pv = pe.tok(1)

                        @OP("tensor")
                        def _(eng, pt=pt, fb=fb, fw=fw, slc=slc, qsl=qsl,
                              ev=evt[fb], vt_ev=(vt_ev if fb == 0 else None),
                              war=(war_pv if fb == 0 else None)):
                            W(eng, ev, vt_ev, war)
                            for kb in range(fb * 4, fb * 4 + fw):
                                mm = eng.matmul(ps_c[:, slc * 128:(slc + 1) * 128],
                                                Vtb[qsl][:, kb * 128:(kb + 1) * 128],
                                                probsT[:, kb * 128:(kb + 1) * 128],
                                                start=(kb == 0), stop=(kb == pt))
                            mm.then_inc(pe.h, 1)

                        last[f"probs{ppar}"] = (pe.h, trk["t_tr"])
                        last[f"diag{ppar}"] = (pe.h, trk["t_tr"])
                        last["probsT"] = (pe.h, t_pv)

                        if pt % 4 == 3:
                            g4 = pt // 4
                            t_pvev = act.tok(1)
                            last["ps_c"] = (act.h, t_pvev)

                            @OP("scalar")
                            def _(eng, t_pv=t_pv, war=last["atst"]):
                                eng.wait_ge(pe.h, t_pv)
                                W(eng, war)
                                eng.copy(out=atst[:], in_=ps_c[:]).then_inc(act.h, 1)

                            t_w = sems["s_attw"].tok(64)
                            last["atst"] = (sems["s_attw"].h, t_w)

                            @OP("sync")
                            def _(eng, h=h, g4=g4, t_pvev=t_pvev):
                                eng.wait_ge(act.h, t_pvev)
                                for i in range(4):
                                    tt = g4 * 4 + i
                                    eng.dma_start(out=attn_scr[tt][:, h, :],
                                                  in_=atst[:, i * 128:(i + 1) * 128]).then_inc(sems["s_attw"].h, 16)

                            if h == H - 1 and g4 == 0:
                                for tp in range(2):
                                    attr_tok[tp] = sems[f"s_attr{tp}"].tok(16)
                                    xc_tok[tp] = sems[f"s_xc{tp}"].tok(16)

                                    @OP("sync")
                                    def _(eng, tp=tp, aw=sems["s_attw"].n,
                                          ph=pe_h14_done, war_x=last[f"xt{tp}"]):
                                        eng.wait_ge(sems["s_attw"].h, aw)
                                        eng.wait_ge(pe.h, ph)
                                        eng.dma_start(out=attC[tp][:].rearrange("p (c n) -> p c n", c=16),
                                                      in_=attn_scr[tp]).then_inc(sems[f"s_attr{tp}"].h, 16)
                                        W(eng, war_x)
                                        eng.dma_start(out=xt[tp], in_=x_d[tp * 128:(tp + 1) * 128, :]).then_inc(sems[f"s_xc{tp}"].h, 16)
                    return emit

                U_units.append(mk_tail())

            order = U_units[:1]
            si, ui = 0, 1
            while si < len(S_units) or ui < len(U_units):
                if si < len(S_units):
                    order.append(S_units[si])
                    si += 1
                if ui < len(U_units):
                    order.append(U_units[ui])
                    ui += 1
            for u in order:
                u()

            if t < NT:
                t_nb = dve.tok(1)

                @OP("vector")
                def _(eng, t=t, nblk=nblk, t_last=mx_toks[-1]):
                    eng.wait_ge(dve.h, t_last)
                    eng.reduce_max(out=st_negb[:, t:t + 1], in_=st_pm[:, 0:nblk],
                                   axis=AX.X, negate=True).then_inc(dve.h, 1)

                t_exp = act.tok(1)
                last["ps_a"] = (act.h, t_exp)

                @OP("scalar")
                def _(eng, t=t, KL=KL, par=par, t_nb=t_nb,
                      war=last[f"probs{par}"]):
                    eng.wait_ge(dve.h, t_nb)
                    W(eng, war)
                    eng.activation(out=probs_b[par][:, 0:KL], in_=ps_a[:, 0:KL],
                                   func=AF.Exp, bias=st_negb[:, t:t + 1],
                                   accum_out=st_ssum[:, t:t + 1]).then_inc(act.h, 1)

                t_rcp = dve.tok(1)

                @OP("vector")
                def _(eng, t=t, t_exp=t_exp, t_rcp=t_rcp):
                    eng.wait_ge(act.h, t_exp)
                    eng.reciprocal(out=st_rcp[:, t:t + 1],
                                   in_=st_ssum[:, t:t + 1]).then_inc(dve.h, 1)

                t_dg = act.tok(1)

                @OP("scalar")
                def _(eng, t=t, par=par, t_rcp=t_rcp, war=last[f"diag{par}"],
                      t_misc=t_misc):
                    eng.wait_ge(dve.h, t_rcp)
                    W(eng, war)
                    eng.wait_ge(sems["s_misc"].h, t_misc)
                    eng.activation(out=diag_b[par][:], in_=ident_b[:],
                                   func=AF.Copy,
                                   scale=st_rcp[:, t:t + 1]).then_inc(act.h, 1)

                if pending and (h < H - 1 or t >= 5):
                    pending.pop(0)()

                pend_prev = (t, par, t_dg)
        for u in pending:
            u()

    barrier(engines=("tensor", "vector", "scalar"))
    xn2_view = xn2h[:].rearrange("p (e n) -> p e n", e=16)
    ln2_tok = {}
    ln2c_apply = {}
    if debug:
        @OP("sync")
        def _(eng, aw=sems["s_attw"].n):
            eng.wait_ge(sems["s_attw"].h, aw)
            eng.dma_start(out=dbg["at"], in_=attn_scr).then_inc(sems["s_dbg"].h, 16)
        sems["s_dbg"].tok(16)

    for t in range(NT):
        sl = t % 2
        if t < 2:
            t_g_last = op_tok[(t, 3)]
            t_ev_last = opadd_tok[(t, 3)]
        else:
            t_xc = sems[f"s_xc{sl}"].tok(16)
            t_ar = sems[f"s_attr{sl}"].tok(16)

            @OP("sync")
            def _(eng, t=t, sl=sl, war_a=last[f"attC{sl}"]):
                W(eng, war_a)
                eng.dma_start(out=attC[sl][:].rearrange("p (c n) -> p c n", c=16),
                              in_=attn_scr[t]).then_inc(sems[f"s_attr{sl}"].h, 16)

            @OP("scalar")
            def _(eng, t=t, sl=sl, war_x=last[f"xt{sl}"]):
                W(eng, war_x)
                eng.dma_start(out=xt[sl], in_=x_d[t * 128:(t + 1) * 128, :]).then_inc(sems[f"s_xc{sl}"].h, 16)

            t_g_last = None
            t_ev_last = None
            for nb in range(4):
                slot = next_psb()
                t_g = pe.tok(1)
                t_g_last = t_g

                @OP("tensor")
                def _(eng, t=t, nb=nb, sl=sl, slot=slot, t_ar=t_ar, t_ow=t_ow,
                      war=last[f"ps_b{slot}"]):
                    eng.wait_ge(sems["s_ow"].h, t_ow)
                    eng.wait_ge(sems[f"s_attr{sl}"].h, t_ar)
                    W(eng, war)
                    for hh in range(16):
                        mm = eng.matmul(ps_b[slot][:],
                                        attC[sl][:, hh * 128:(hh + 1) * 128],
                                        bigA_bf[:, hh * 2048 + nb * 512: hh * 2048 + nb * 512 + 512],
                                        start=(hh == 0), stop=(hh == 15))
                    mm.then_inc(pe.h, 1)

                t_ev = dve.tok(1)
                last[f"ps_b{slot}"] = (dve.h, t_ev)
                t_ev_last = t_ev

                @OP("vector")
                def _(eng, t=t, nb=nb, sl=sl, slot=slot, t_g=t_g, t_xc=t_xc,
                      war=last[f"x1t{sl}"]):
                    eng.wait_ge(pe.h, t_g)
                    eng.wait_ge(sems[f"s_xc{sl}"].h, t_xc)
                    W(eng, war)
                    eng.tensor_tensor(out=x1t[sl][:, nb * 512:(nb + 1) * 512],
                                      in0=ps_b[slot][:], in1=xt[sl][:, nb * 512:(nb + 1) * 512],
                                      op=ALU.add).then_inc(dve.h, 1)

        last[f"attC{sl}"] = (pe.h, t_g_last)
        last[f"xt{sl}"] = (dve.h, t_ev_last)
        t_x1 = sems[f"s_x1w{sl}"].tok(16)
        last[f"x1t{sl}"] = (sems[f"s_x1w{sl}"].h, t_x1)

        @OP("gpsimd")
        def _(eng, t=t, sl=sl, t_ev=t_ev_last):
            eng.wait_ge(dve.h, t_ev)
            eng.dma_start(out=x1_scr[t * 128:(t + 1) * 128, :], in_=x1t[sl][:]).then_inc(sems[f"s_x1w{sl}"].h, 16)

        if t < 8:
            t_ap2 = emit_ln_tile(t, x1t[sl], t_ev_last, dve.h, xn2_view,
                                 t * 128)
            ln2c_apply[sl] = (dve.h, t_ap2)
    ln2_tok[0] = act.n

    if debug:
        @OP("sync")
        def _(eng, n0=sems["s_x1w0"].n, n1=sems["s_x1w1"].n):
            eng.wait_ge(sems["s_x1w0"].h, n0)
            eng.wait_ge(sems["s_x1w1"].h, n1)
            eng.dma_start(out=dbg["x1"], in_=x1_scr).then_inc(sems["s_dbg"].h, 16)
        sems["s_dbg"].tok(16)

    barrier(extra=(sems["s_x1w0"], sems["s_x1w1"]))

    y_toks = {}
    for hf in range(2):
        toff = hf * 8

        dai_tok = {}
        for i in range(8):
            tt = toff + i
            dai_tok[i] = sems["s_dai"].tok(16)

            @OP("sync")
            def _(eng, tt=tt, i=i, war_y=(y_toks.get(i) if hf else None)):
                if war_y:
                    eng.wait_ge(sems["s_y"].h, war_y)
                eng.dma_start(out=bigA[:, i * 2048:(i + 1) * 2048],
                              in_=x1_scr[tt * 128:(tt + 1) * 128, :]).then_inc(sems["s_dai"].h, 16)

        ln2_act = ln2_tok[hf]

        for g in range(NGRP):
            t_mul_last = None
            for hcg in range(HGRP):
                hc = g * HGRP + hcg
                wsl = hc % 2
                t_gww = sems[f"s_gw{wsl}"].tok(16)
                t_uww = sems[f"s_uw{wsl}"].tok(16)
                sems["s_dw"].tok(16)

                @OP("gpsimd")
                def _(eng, hc=hc, hcg=hcg, wsl=wsl, war_g=last[f"gw{wsl}"],
                      war_u=last[f"uw{wsl}"], war_d=last["dwg"] if hcg == 0 else None):
                    W(eng, war_g)
                    eng.dma_start(out=gwb[wsl][:].rearrange("p (e n) -> p e n", e=16),
                                  in_=gw_d[hc]).then_inc(sems[f"s_gw{wsl}"].h, 16)
                    W(eng, war_u)
                    eng.dma_start(out=uwb[wsl][:].rearrange("p (e n) -> p e n", e=16),
                                  in_=uw_d[hc]).then_inc(sems[f"s_uw{wsl}"].h, 16)
                    if war_d:
                        W(eng, war_d)
                    eng.dma_start(out=dwg[:, hcg * 2048:(hcg + 1) * 2048],
                                  in_=dw_d[hc]).then_inc(sems["s_dw"].h, 16)

                gts, uts = [], []
                for ch in range(2):
                    go = ch * 1024
                    uo = ch * 1024 + 512
                    t_gate = pe.tok(1)
                    gts.append(t_gate)

                    @OP("tensor")
                    def _(eng, ch=ch, wsl=wsl, go=go, t_w=t_gww,
                          ln2_act=ln2_act, war=last[f"ps_f{ch}"]):
                        eng.wait_ge(sems[f"s_gw{wsl}"].h, t_w)
                        eng.wait_ge(act.h, ln2_act)
                        W(eng, war)
                        for k in range(16):
                            mm = eng.matmul(ps_a[:, go:go + 512],
                                            gwb[wsl][:, k * 128:(k + 1) * 128],
                                            xn2h[:, k * 1024 + ch * 512: k * 1024 + ch * 512 + 512],
                                            start=(k == 0), stop=(k == 15))
                        mm.then_inc(pe.h, 1)

                    t_up = pe.tok(1)
                    uts.append(t_up)

                    @OP("tensor")
                    def _(eng, ch=ch, wsl=wsl, uo=uo, t_w=t_uww,
                          war=last[f"ps_f{ch}"]):
                        eng.wait_ge(sems[f"s_uw{wsl}"].h, t_w)
                        W(eng, war)
                        for k in range(16):
                            mm = eng.matmul(ps_a[:, uo:uo + 512],
                                            uwb[wsl][:, k * 128:(k + 1) * 128],
                                            xn2h[:, k * 1024 + ch * 512: k * 1024 + ch * 512 + 512],
                                            start=(k == 0), stop=(k == 15))
                        mm.then_inc(pe.h, 1)

                    t_sil = act.tok(1)

                    @OP("scalar")
                    def _(eng, ch=ch, go=go, t_gate=t_gate, war=last[f"sil{ch}"]):
                        eng.wait_ge(pe.h, t_gate)
                        W(eng, war)
                        eng.activation(out=sil[:, ch * 512:(ch + 1) * 512],
                                       in_=ps_a[:, go:go + 512],
                                       func=AF.Silu).then_inc(act.h, 1)

                    t_mul = dve.tok(1)
                    t_mul_last = t_mul
                    last[f"sil{ch}"] = (dve.h, t_mul)
                    last[f"ps_f{ch}"] = (dve.h, t_mul)

                    @OP("vector")
                    def _(eng, hcg=hcg, ch=ch, uo=uo, t_up=t_up, t_sil=t_sil,
                          war=last["hid"] if (hcg == 0 and ch == 0) else None):
                        eng.wait_ge(pe.h, t_up)
                        eng.wait_ge(act.h, t_sil)
                        if war:
                            W(eng, war)
                        eng.tensor_tensor(out=hid[:, hcg * 1024 + ch * 512: hcg * 1024 + ch * 512 + 512],
                                          in0=sil[:, ch * 512:(ch + 1) * 512],
                                          in1=ps_a[:, uo:uo + 512],
                                          op=ALU.mult).then_inc(dve.h, 1)

                last[f"gw{wsl}"] = (pe.h, gts[1])
                last[f"uw{wsl}"] = (pe.h, uts[1])
                last_up_tok = uts[1]

            t_dw_w = sems["s_dw"].n
            t_dn_last = None
            t_add_by = {}
            for ti in range(8):
                for nb in range(4):
                    slot = next_psb()
                    t_dn = pe.tok(1)
                    t_dn_last = t_dn

                    @OP("tensor")
                    def _(eng, ti=ti, nb=nb, slot=slot, t_dw_w=t_dw_w,
                          t_mul=t_mul_last, war=last[f"ps_b{slot}"]):
                        eng.wait_ge(sems["s_dw"].h, t_dw_w)
                        eng.wait_ge(dve.h, t_mul)
                        W(eng, war)
                        for hcg in range(HGRP):
                            mm = eng.matmul(ps_b[slot][:],
                                            hid[:, hcg * 1024 + ti * 128: hcg * 1024 + ti * 128 + 128],
                                            dwg[:, hcg * 2048 + nb * 512: hcg * 2048 + nb * 512 + 512],
                                            start=(hcg == 0), stop=(hcg == HGRP - 1))
                        mm.then_inc(pe.h, 1)

                    t_add = dve.tok(1)
                    last[f"ps_b{slot}"] = (dve.h, t_add)
                    t_add_by[ti] = t_add

                    @OP("vector")
                    def _(eng, ti=ti, nb=nb, slot=slot, t_dn=t_dn, t_dai=dai_tok[ti]):
                        eng.wait_ge(pe.h, t_dn)
                        eng.wait_ge(sems["s_dai"].h, t_dai)
                        dst = bigA[:, ti * 2048 + nb * 512: ti * 2048 + (nb + 1) * 512]
                        eng.tensor_tensor(out=dst, in0=dst, in1=ps_b[slot][:],
                                          op=ALU.add).then_inc(dve.h, 1)

                if hf == 0 and g == NGRP - 1:
                    tt2 = 8 + ti
                    sl2 = ti % 2
                    t_x1r = sems[f"s_x1r{sl2}"].tok(16)

                    @OP("sync")
                    def _(eng, tt2=tt2, sl2=sl2, war=last[f"x1t{sl2}"],
                          war2=ln2c_apply.get(sl2)):
                        W(eng, war, war2)
                        eng.dma_start(out=x1t[sl2], in_=x1_scr[tt2 * 128:(tt2 + 1) * 128, :]).then_inc(sems[f"s_x1r{sl2}"].h, 16)

                    t_ap = emit_ln_tile(ti, x1t[sl2], t_x1r,
                                        sems[f"s_x1r{sl2}"].h, xn2_view,
                                        ti * 128, dst_war=(pe.h, last_up_tok))
                    last[f"x1t{sl2}"] = (dve.h, t_ap)

            last["hid"] = (pe.h, t_dn_last)
            last["dwg"] = (pe.h, t_dn_last)
            if g == NGRP - 1:
                final_adds = dict(t_add_by)
        if hf == 0:
            ln2_tok[1] = act.n

        for ti in range(8):
            tt = toff + ti
            y_toks[ti] = sems["s_y"].tok(16)

            @OP("sync")
            def _(eng, tt=tt, ti=ti, t_add=final_adds[ti]):
                eng.wait_ge(dve.h, t_add)
                eng.dma_start(out=y_d[tt * 128:(tt + 1) * 128, :],
                              in_=bigA[:, ti * 2048:(ti + 1) * 2048]).then_inc(sems["s_y"].h, 16)

    @OP("sync")
    def _(eng, n=sems["s_y"].n, nd=sems["s_dbg"].n):
        eng.wait_ge(sems["s_y"].h, n)
        if nd:
            eng.wait_ge(sems["s_dbg"].h, nd)

    with stack:
        with nc.Block() as block:
            def runner(ename):
                def f(eng):
                    for fn in steps[ename]:
                        fn(eng)
                return f
            block.sync(runner("sync"))
            block.tensor(runner("tensor"))
            block.vector(runner("vector"))
            block.scalar(runner("scalar"))
            block.gpsimd(runner("gpsimd"))
    return nc


def _prep_inputs(x, q_wei, k_wei, v_wei, out_wei, w_gate, w_up, w_down):
    asc = np.ascontiguousarray
    qw = asc(np.asarray(q_wei, np.float32).reshape(16, 128, 16, 128).transpose(2, 1, 0, 3))
    kw = asc(np.asarray(k_wei, np.float32).reshape(16, 128, 16, 128).transpose(2, 1, 0, 3))
    vw = asc(np.asarray(v_wei, np.float32).reshape(16, 128, 16, 128).transpose(2, 1, 0, 3))
    ow = asc(np.asarray(out_wei, np.float32))
    gw = asc(np.asarray(w_gate, np.float32).reshape(64, 128, 16, 128).transpose(0, 3, 2, 1))
    uw = asc(np.asarray(w_up, np.float32).reshape(64, 128, 16, 128).transpose(0, 3, 2, 1))
    dw = asc(np.asarray(w_down, np.float32).T.reshape(64, 128, 2048))
    mask = np.triu(np.full((128, 128), -66504.0, dtype=np.float32), k=1)
    ident = np.eye(128, dtype=np.float32)
    x = np.asarray(x, np.float32)
    maps = []
    for c in range(8):
        maps.append(dict(x=asc(x[c]), qw=qw, kw=kw, vw=vw, ow=ow,
                         gw=gw, uw=uw, dw=dw, mask=mask, ident=ident))
    return maps


def kernel(L0_attn_hidden_states, L0_attn_q_wei, L0_attn_k_wei, L0_attn_v_wei,
           L0_attn_out_wei, L0_ffn_w_gate, L0_ffn_w_up, L0_ffn_w_down):
    maps = _prep_inputs(L0_attn_hidden_states, L0_attn_q_wei, L0_attn_k_wei,
                        L0_attn_v_wei, L0_attn_out_wei, L0_ffn_w_gate,
                        L0_ffn_w_up, L0_ffn_w_down)
    nc = build(debug=False)
    res = run_bass_kernel_spmd(nc, maps, core_ids=list(range(8)))
    return np.stack([np.asarray(r["y"]) for r in res.results], axis=0)
